# version 05
# baseline (speedup 1.0000x reference)
import contextlib
import math
import numpy as np
import ml_dtypes
import concourse.bass as bass
import concourse.mybir as mybir
from concourse.bass_utils import run_bass_kernel_spmd

F32 = mybir.dt.float32
BF16 = mybir.dt.bfloat16
AF = mybir.ActivationFunctionType
ALU = mybir.AluOpType
AX = mybir.AxisListType


ENGINES = ("PE", "ACT", "DVE", "POOL", "SP")
_BLOCK_ATTR = {"PE": "tensor", "ACT": "scalar", "DVE": "vector",
               "POOL": "gpsimd", "SP": "sync"}
_QUEUE_ENG = {"qSP": "SP", "qACT": "ACT", "qPOOL": "POOL"}


def _key_of(x):
    if isinstance(x, bass.AP):
        return (x.tensor.name, "*")
    if isinstance(x, tuple) and len(x) == 2:
        return x
    if isinstance(x, str):
        return (x, "*")
    if hasattr(x, "name"):
        return (x.name, "*")
    raise TypeError(f"bad dep key {x!r}")


class Emitter:
    class Op:
        __slots__ = ("sem", "val", "clock")
        def __init__(self, sem, val, clock):
            self.sem, self.val, self.clock = sem, val, clock

    def __init__(self, nc, stack, n_dma_sems=80):
        self.nc = nc
        self.sem = {}
        for name in ENGINES + ("CC",):
            self.sem[name] = stack.enter_context(nc.semaphore("sem_" + name))
        self.cnt = {name: 0 for name in ENGINES + ("CC",)}
        self.hb = {e: {} for e in ENGINES}
        self.writes = {}
        self.reads = {}
        self._free_sems = [stack.enter_context(nc.semaphore(f"dsem_{i}"))
                           for i in range(n_dma_sems)]
        self._chan_last = {}
        self._rec = None
        self._freed = []
        self._seed = {}
        self.n_ops = 0
        self.n_waits = 0

    def _need(self, eng, dep, waits):
        if dep is None:
            return
        hb = self.hb[eng]
        if hb.get(dep.sem, 0) >= dep.val:
            return
        waits.append((dep.sem, dep.val))
        hb[dep.sem] = dep.val
        for s, v in dep.clock.items():
            if hb.get(s, 0) < v:
                hb[s] = v

    def _dep_ops(self, key, for_write):
        tname, sub = key
        wmap = self.writes.get(tname, {})
        rmap = self.reads.get(tname, {})
        deps = list(self._seed.get(tname, ()))
        if sub == "*":
            deps.extend(wmap.values())
            if for_write:
                for lst in rmap.values():
                    deps.extend(lst)
        else:
            if sub in wmap:
                deps.append(wmap[sub])
            if "*" in wmap:
                deps.append(wmap["*"])
            if for_write:
                deps.extend(rmap.get(sub, ()))
                deps.extend(rmap.get("*", ()))
        return deps

    def _note_read(self, key, op):
        tname, sub = key
        self.reads.setdefault(tname, {}).setdefault(sub, []).append(op)

    def _note_write(self, key, op):
        tname, sub = key
        wmap = self.writes.setdefault(tname, {})
        rmap = self.reads.setdefault(tname, {})
        if sub == "*":
            wmap.clear()
            rmap.clear()
        else:
            rmap.pop(sub, None)
        wmap[sub] = op

    @contextlib.contextmanager
    def block(self, name=None):
        assert self._rec is None
        self._rec = {e: [] for e in ENGINES}
        yield self
        rec = self._rec
        self._rec = None
        with self.nc.Block(name) as blk:
            for eng in ENGINES:
                items = rec[eng]
                if not items:
                    continue
                def replay(h, items=items):
                    for fn in items:
                        fn(h)
                getattr(blk, _BLOCK_ATTR[eng])(replay)

    def _record(self, eng, fn):
        assert self._rec is not None, "emit outside of block()"
        self._rec[eng].append(fn)

    def op(self, eng, build, reads=(), writes=(), sem=None, inc=1):
        return self.ops(eng, [build], reads=reads, writes=writes, sem=sem,
                        inc=inc)

    def ops(self, eng, builds, reads=(), writes=(), sem=None, inc=1):
        sem_name = sem or eng
        rkeys, wkeys, waits = [], [], []
        for x in reads:
            if isinstance(x, Emitter.Op):
                self._need(eng, x, waits)
            else:
                rkeys.append(_key_of(x))
        for x in writes:
            if isinstance(x, Emitter.Op):
                self._need(eng, x, waits)
            else:
                wkeys.append(_key_of(x))
        for k in rkeys:
            for dep in self._dep_ops(k, for_write=False):
                self._need(eng, dep, waits)
        for k in wkeys:
            for dep in self._dep_ops(k, for_write=True):
                self._need(eng, dep, waits)
        self.cnt[sem_name] += inc
        val = self.cnt[sem_name]
        clock = dict(self.hb[eng])
        clock[sem_name] = val
        me = Emitter.Op(sem_name, val, clock)
        for k in rkeys:
            self._note_read(k, me)
        for k in wkeys:
            self._note_write(k, me)
        sem_handles = self.sem
        def fn(h, waits=tuple(waits), builds=tuple(builds),
               sem_h=sem_handles[sem_name], inc=inc):
            for s, v in waits:
                h.wait_ge(sem_handles[s], v)
            inst = None
            for b in builds:
                inst = b(h)
                assert inst is not None, f"builder returned None: {b.__code__.co_filename}:{b.__code__.co_firstlineno}"
            inst.then_inc(sem_h, inc)
        self._record(eng, fn)
        self.n_ops += len(builds)
        self.n_waits += len(waits)
        return me

    @contextlib.contextmanager
    def buffers(self):
        scope = _BufScope(self)
        try:
            yield scope
        finally:
            scope.close()

    def _tensor_ops(self, name):
        ops = []
        for op in self.writes.get(name, {}).values():
            ops.append(op)
        for lst in self.reads.get(name, {}).values():
            ops.extend(lst)
        return ops

    def _chan_sem(self, name):
        if name not in self.sem:
            assert self._free_sems, "out of DMA channel semaphores"
            self.sem[name] = self._free_sems.pop()
            self.cnt[name] = 0
        return name

    def dma(self, queue, out, in_, reads=(), writes=(), ch=None, inc=16, **kw):
        eng = _QUEUE_ENG[queue]
        sem_name = self._chan_sem(f"{queue}.{ch or 'd'}")
        prev = self._chan_last.get(sem_name)
        rd = (list(reads) + [in_]) if not reads else list(reads)
        if prev is not None:
            rd = rd + [prev]
        me = self.op(
            eng,
            lambda h: h.dma_start(out=out, in_=in_, **kw),
            reads=rd,
            writes=(list(writes) + [out]) if not writes else writes,
            sem=sem_name, inc=inc)
        self._chan_last[sem_name] = me
        return me

    def dmat(self, queue, out, in_, reads=(), writes=(), ch=None):
        eng = _QUEUE_ENG[queue]
        assert eng in ("SP", "ACT")
        sem_name = self._chan_sem(f"{queue}.{ch or 'dt'}")
        prev = self._chan_last.get(sem_name)
        rd = (list(reads) + [in_]) if not reads else list(reads)
        if prev is not None:
            rd = rd + [prev]
        me = self.op(
            eng,
            lambda h: h.dma_start(out=out, in_=in_, transpose=True),
            reads=rd,
            writes=(list(writes) + [out]) if not writes else writes,
            sem=sem_name, inc=16)
        self._chan_last[sem_name] = me
        return me

    def cc(self, build, reads=(), writes=(), inc=1):
        prev = self._chan_last.get("CC")
        rd = list(reads) + ([prev] if prev is not None else [])
        me = self.op("POOL", build, reads=rd, writes=writes, sem="CC", inc=inc)
        self._chan_last["CC"] = me
        return me

    def wait_all(self, eng_list=None):
        for eng in (eng_list or ["SP"]):
            waits = []
            for tname, wmap in self.writes.items():
                for dep in wmap.values():
                    self._need(eng, dep, waits)
            def fn(h, waits=tuple(waits), sems=self.sem):
                for s, v in waits:
                    h.wait_ge(sems[s], v)
            self._record(eng, fn)
            self.n_waits += len(waits)



class _BufScope:
    def __init__(self, P):
        self.P = P
        self._items = []

    def sbuf(self, name, shape, dtype):
        P = self.P
        nc = P.nc
        lo0, hi0 = nc.sbuf_base, nc.sbuf_top
        ctx = nc.sbuf_tensor(name, shape, dtype)
        ten = ctx.__enter__()
        lo1, hi1 = nc.sbuf_base, nc.sbuf_top
        rng = (lo0, lo1) if lo1 != lo0 else (hi1, hi0)
        seeds = []
        for (flo, fhi), ops in P._freed:
            if flo < rng[1] and rng[0] < fhi:
                seeds.extend(ops)
        if seeds:
            P._seed[ten.name] = seeds
        self._items.append((ctx, ten.name, rng))
        return ten

    def close(self):
        P = self.P
        for ctx, name, rng in reversed(self._items):
            ops = P._tensor_ops(name)
            if ops:
                P._freed.append((rng, ops))
            P.writes.pop(name, None)
            P.reads.pop(name, None)
            P._seed.pop(name, None)
            ctx.__exit__(None, None, None)


class Cfg:
    def __init__(self, T=4096):
        self.T = T
        self.D = 2048
        self.HK = 16
        self.DK = 128
        self.NH = 8
        self.NE = 8
        self.E = 16
        self.I = 1408
        self.SI = 1408
        self.C = 128
        self.TH = T // 2
        self.NC = T // 128
        self.NQ = T // 512
        self.ARCH = min(1024, T)
        self.NAR = T // self.ARCH
        self.eps_ln = 1e-5
        self.eps_l2 = 1e-6
        self.eps_rms = 1e-5



def _f32c(a):
    return np.ascontiguousarray(a, dtype=np.float32)


def stage_inputs(inp, cfg):
    T, D = cfg.T, cfg.D
    x = np.asarray(inp["L0_attn_a_hidden_states"])
    W_qkv = np.asarray(inp["L0_attn_a_W_qkv"])
    W_ab = np.asarray(inp["L0_attn_a_W_ab"])
    conv_w = np.asarray(inp["L0_attn_a_conv_w"])
    A_log = np.asarray(inp["L0_attn_a_A_log"])
    dt_bias = np.asarray(inp["L0_attn_a_dt_bias"])
    o_norm_w = np.asarray(inp["L0_attn_a_o_norm_w"])
    W_o = np.asarray(inp["L0_attn_a_W_o"])
    w_attn = np.asarray(inp["L0_attn_b_w_attn"])
    b_attn = np.asarray(inp["L0_attn_b_b_attn"])
    w_proj = np.asarray(inp["L0_attn_b_w_proj"])
    b_proj = np.asarray(inp["L0_attn_b_b_proj"])
    gate_weight = np.asarray(inp["L0_ffn_gate_weight"])
    e_bias = np.asarray(inp["L0_ffn_e_score_correction_bias"])
    gate_proj = np.asarray(inp["L0_ffn_gate_proj"])
    up_proj = np.asarray(inp["L0_ffn_up_proj"])
    down_proj = np.asarray(inp["L0_ffn_down_proj"])
    sh_gate = np.asarray(inp["L0_ffn_shared_gate_w"])
    sh_up = np.asarray(inp["L0_ffn_shared_up_w"])
    sh_down = np.asarray(inp["L0_ffn_shared_down_w"])

    ident = np.eye(128, dtype=np.float32)
    tri = np.triu(np.ones((128, 128), np.float32), 1)
    mask_strict = _f32c((tri - 1.0) * 1e30)
    cm = np.zeros((4, 128, 512), np.float32)
    for a in range(4):
        for tk in range(128):
            cm[a, tk, : a * 128 + tk] = -1e30
    causal_mask = _f32c(cm.reshape(512, 512))

    def blobA(gp, up):
        g = gp.reshape(11, 128, 16, 128).transpose(3, 0, 2, 1)
        u = up.reshape(11, 128, 16, 128).transpose(3, 0, 2, 1)
        return np.stack([g, u], axis=2)

    def blobB(dp):
        return dp.reshape(2048, 11, 128).transpose(2, 1, 0)

    in_maps = []
    for c in range(8):
        b, r = c // 2, c % 2
        own_h = list(range(8 * r, 8 * r + 8))
        perm = own_h + [h for h in range(16) if h not in own_h]
        qcols = np.concatenate([np.arange(h * 128, h * 128 + 128) for h in own_h])
        d = {}
        d["xT"] = _f32c(x[b, :T].T)
        d["w_qkv"] = _f32c(np.concatenate(
            [W_qkv[:, qcols], W_qkv[:, 2048 + qcols], W_qkv[:, 4096 + qcols]],
            axis=1))
        d["conv_w"] = _f32c(np.concatenate(
            [conv_w[qcols], conv_w[2048 + qcols], conv_w[4096 + qcols]], axis=0))
        w128 = np.zeros((2048, 128), np.float32)
        w128[:, 0:8] = W_ab[:, own_h]
        w128[:, 32:40] = W_ab[:, [16 + h for h in own_h]]
        w128[:, 64:80] = gate_weight[perm].T
        d["w_ab128"] = _f32c(w128)
        d["head_vec"] = _f32c(np.stack(
            [np.concatenate([A_log[own_h], np.zeros(8, np.float32)]),
             np.concatenate([dt_bias[own_h], np.zeros(8, np.float32)])],
            axis=0).T)
        d["w_gate"] = _f32c(W_ab[:, 32 + qcols])
        d["o_norm_w"] = _f32c(o_norm_w.reshape(1, 128))
        d["w_o"] = _f32c(W_o[qcols])
        d["w_attn"] = _f32c(np.concatenate(
            [w_attn[:, qcols], w_attn[:, 2048 + qcols], w_attn[:, 4096 + qcols]],
            axis=1))
        d["b_attn"] = _f32c(np.concatenate(
            [b_attn[qcols], b_attn[2048 + qcols], b_attn[4096 + qcols]]
        ).reshape(24, 128).T)
        d["w_proj"] = _f32c(w_proj[qcols])
        d["b_proj"] = _f32c(b_proj.reshape(16, 128).T if r == 0
                            else np.zeros((128, 16), np.float32))
        d["router_bias"] = _f32c(e_bias[perm].reshape(1, 16))
        d["tie_eps"] = _f32c(np.array([-1e-30 * j for j in perm]).reshape(1, 16))
        eA = [blobA(gate_proj[e], up_proj[e]) for e in [perm[i] for i in range(8)]]
        eA.append(blobA(sh_gate[r * 1408:(r + 1) * 1408],
                        sh_up[r * 1408:(r + 1) * 1408]))
        d["expA"] = _f32c(np.stack(eA))
        eB = [blobB(down_proj[e]) for e in [perm[i] for i in range(8)]]
        eB.append(blobB(sh_down[:, r * 1408:(r + 1) * 1408]))
        d["expB"] = _f32c(np.stack(eB))
        d["ident"] = ident
        sel = np.zeros((64, 16 * 128), np.float32)
        for j in range(16):
            row = j if j < 8 else 32 + (j - 8)
            sel[row, j * 128:(j + 1) * 128] = 1.0
        d["sel16"] = sel
        d["mask_strict"] = mask_strict
        d["causal_mask"] = causal_mask
        in_maps.append(d)
    return in_maps



class B:
    pass


def declare(nc, cfg, taps):
    T, D = cfg.T, cfg.D
    t = B()
    t.taps = {}
    di = lambda n, s: nc.dram_tensor(n, s, F32, kind="ExternalInput")
    t.xT = di("xT", [D, T])
    t.w_qkv = di("w_qkv", [D, 3072])
    t.conv_w = di("conv_w", [3072, 4])
    t.w_ab128 = di("w_ab128", [D, 128])
    t.head_vec = di("head_vec", [16, 2])
    t.w_gate = di("w_gate", [D, 1024])
    t.o_norm_w = di("o_norm_w", [1, 128])
    t.w_o = di("w_o", [1024, D])
    t.w_attn = di("w_attn", [D, 3072])
    t.b_attn = di("b_attn", [128, 24])
    t.w_proj = di("w_proj", [1024, D])
    t.b_proj = di("b_proj", [128, 16])
    t.router_bias = di("router_bias", [1, 16])
    t.tie_eps = di("tie_eps", [1, 16])
    t.expA = di("expA", [9, 128, 11, 2, 16, 128])
    t.expB = di("expB", [9, 128, 11, 2048])
    t.ident = di("ident", [128, 128])
    t.sel16 = di("sel16", [64, 16 * 128])
    t.mask_strict = di("mask_strict", [128, 128])
    t.causal_mask = di("causal_mask", [512, 512])

    t.out_xT = nc.dram_tensor("out_xT", [D, T], F32, kind="ExternalOutput")

    ds = lambda n, s, dt=F32: nc.dram_tensor(n, s, dt)
    t.xA = ds("xA", [D, T])
    t.xB = ds("xB", [D, T])
    t.hsT = ds("hsT", [D, T], BF16)
    t.qkvT = ds("qkvT", [3072, T], BF16)
    t.qkvT2 = ds("qkvT2", [3072, T], BF16)
    t.gateT = ds("gateT", [1024, T], BF16)
    t.oT = ds("oT", [1024, T], BF16)
    t.ar_in = ds("ar_in", [cfg.NAR, D, cfg.ARCH])
    t.ar_out = ds("ar_out", [cfg.NAR, D, cfg.ARCH])
    t.rows_dram = ds("rows_dram", [64, T])
    return t


def alloc_common(nc, st, cfg, t):
    E = st.enter_context
    T = cfg.T
    t.cst = E(nc.sbuf_tensor("cst", [128, 8], F32))
    t.ones_col = E(nc.sbuf_tensor("ones_col", [128, 1], BF16))
    t.ones_row1 = E(nc.sbuf_tensor("ones_row1", [1, 128], F32))
    t.ones_row1b = E(nc.sbuf_tensor("ones_row1b", [1, 128], BF16))
    t.ident_bf = E(nc.sbuf_tensor("ident_bf", [128, 128], BF16))
    t.ident_f = E(nc.sbuf_tensor("ident_f", [128, 128], F32))
    t.mask_strict_sb = E(nc.sbuf_tensor("mask_strict_sb", [128, 128], F32))
    W = 512 if T >= 512 else T
    t.row_mu = E(nc.sbuf_tensor("row_mu", [1, 2 * W], F32))
    t.row_rs = E(nc.sbuf_tensor("row_rs", [1, 2 * W], F32))
    t.row_murs = E(nc.sbuf_tensor("row_murs", [1, 2 * W], F32))
    t.sel16_sb = E(nc.sbuf_tensor("sel16_sb", [64, 16 * 128], F32))
    t.b_attn_sb = E(nc.sbuf_tensor("b_attn_sb", [128, 24], F32))
    t.rs_repw = [E(nc.sbuf_tensor(f"rs_repw{i}", [128, W], F32)) for i in range(2)]
    t.murs_repw = [E(nc.sbuf_tensor(f"murs_repw{i}", [128, W], F32)) for i in range(2)]
    t.pb = [E(nc.psum_tensor(f"pb{i}", [128, 512], F32)) for i in range(8)]
    t.ps_row_x = t.pb[4]
    t.ps_row_s = t.pb[5]
    t.ps_rep = t.pb[6]
    t.ps_ab = t.pb[7]


def emit_consts(P, t):
    P.op("POOL", lambda h: h.memset(t.cst[:, 0:1], 1e-5), writes=[(t.cst.name, 0)])
    P.op("POOL", lambda h: h.memset(t.cst[:, 1:2], 1e-6), writes=[(t.cst.name, 1)])
    P.op("POOL", lambda h: h.memset(t.cst[:, 2:3], 1e-5), writes=[(t.cst.name, 2)])
    P.op("POOL", lambda h: h.memset(t.cst[:, 3:4], 1.0), writes=[(t.cst.name, 3)])
    P.op("POOL", lambda h: h.memset(t.cst[:, 4:5], -0.5 * math.log(128.0)),
         writes=[(t.cst.name, 4)])
    P.op("POOL", lambda h: h.memset(t.cst[:, 5:6], 0.0), writes=[(t.cst.name, 5)])
    P.op("POOL", lambda h: h.memset(t.ones_col[:, :], 1.0), writes=[t.ones_col])
    P.op("POOL", lambda h: h.memset(t.ones_row1[:, :], 1.0), writes=[t.ones_row1])
    P.op("POOL", lambda h: h.memset(t.ones_row1b[:, :], 1.0), writes=[t.ones_row1b])
    P.dma("qPOOL", t.ident_bf[:, :], t.ident[:, :], ch="c0")
    P.dma("qPOOL", t.sel16_sb[:, :], t.sel16[:, :], ch="c0")
    P.dma("qSP", t.b_attn_sb[:, :], t.b_attn[:, :], ch="c1")
    P.dma("qSP", t.ident_f[:, :], t.ident[:, :], ch="c1")
    P.dma("qSP", t.mask_strict_sb[:, :], t.mask_strict[:, :], ch="c1")


def emit_ln(P, nc, cfg, t, src, hs_dst, tag, ab=None):
    T, D = cfg.T, cfg.D
    W = 512 if T >= 512 else T
    NW = T // W
    src3 = src.ap().rearrange("(c p) t -> p c t", p=128)
    hs3 = hs_dst.ap().rearrange("(c p) t -> p c t", p=128)

    with P.block(f"ln_{tag}"), P.buffers() as bf:
        _xb = bf.sbuf(f"{tag}_xbw0", [128, 16 * W], BF16)
        xbw = [_xb, _xb]
        _sq = bf.sbuf(f"{tag}_sqw0", [128, 16 * W], BF16)
        sqw = [_sq, _sq]
        xfw = [bf.sbuf(f"{tag}_xfw{i}", [128, 16 * W], F32) for i in range(2)]
        hsww = [bf.sbuf(f"{tag}_hsw{i}", [128, 16 * W], BF16) for i in range(2)]
        for w in range(NW):
            w0 = w * W
            sl = w % 2
            xb = xbw[sl]
            xb3 = xb[:, :].rearrange("p (c t) -> p c t", c=16)
            P.dma("qPOOL", xb3, src3[:, :, w0:w0 + W],
                  ch=f"xbw{sl}", writes=[xb])
            sq = sqw[sl]
            sq3 = sq[:, :].rearrange("p (c t) -> p c t", c=16)
            P.op("DVE", lambda h, xb=xb, sq=sq: h.tensor_mul(
                    sq[:, :], xb[:, :], xb[:, :]), reads=[xb], writes=[sq])
            P.ops("PE", [lambda h, dc=dc, xb3=xb3: h.matmul(
                    t.ps_row_x[0:1, 0:W], t.ones_col[:, :], xb3[:, dc, :],
                    start=(dc == 0), stop=(dc == 15)) for dc in range(16)],
                  reads=[xb, t.ones_col], writes=[t.ps_row_x])
            P.ops("PE", [lambda h, dc=dc, sq3=sq3: h.matmul(
                    t.ps_row_s[0:1, 0:W], t.ones_col[:, :], sq3[:, dc, :],
                    start=(dc == 0), stop=(dc == 15)) for dc in range(16)],
                  reads=[sq, t.ones_col], writes=[t.ps_row_s])
            mu = t.row_mu[0:1, sl * W:sl * W + W]
            rs = t.row_rs[0:1, sl * W:sl * W + W]
            murs = t.row_murs[0:1, sl * W:sl * W + W]
            P.op("DVE", lambda h, mu=mu: h.tensor_scalar_mul(
                    mu, t.ps_row_x[0:1, 0:W], 1.0 / D),
                 reads=[t.ps_row_x], writes=[(t.row_mu.name, sl)])
            P.op("DVE", lambda h, mu=mu, murs=murs: h.tensor_mul(
                    murs, mu, mu),
                 reads=[(t.row_mu.name, sl)], writes=[(t.row_murs.name, f"t{sl}")])
            P.op("DVE", lambda h, rs=rs, murs=murs: h.scalar_tensor_tensor(
                    rs, t.ps_row_s[0:1, 0:W], 1.0 / D, murs,
                    ALU.mult, ALU.subtract),
                 reads=[t.ps_row_s, (t.row_murs.name, f"t{sl}")],
                 writes=[(t.row_rs.name, f"v{sl}")])
            P.op("ACT", lambda h, rs=rs: h.activation(
                    rs, rs, AF.Ln, bias=t.cst[0:1, 0:1]),
                 reads=[(t.row_rs.name, f"v{sl}"), (t.cst.name, 0)],
                 writes=[(t.row_rs.name, f"l{sl}")])
            P.op("ACT", lambda h, rs=rs: h.activation(
                    rs, rs, AF.Exp, scale=-0.5),
                 reads=[(t.row_rs.name, f"l{sl}")], writes=[(t.row_rs.name, sl)])
            P.op("DVE", lambda h, mu=mu, rs=rs, murs=murs: h.tensor_mul(
                    murs, mu, rs),
                 reads=[(t.row_mu.name, sl), (t.row_rs.name, sl)],
                 writes=[(t.row_murs.name, sl)])
            rsw = t.rs_repw[sl]
            mursw = t.murs_repw[sl]
            P.op("PE", lambda h, rs=rs: h.matmul(
                    t.ps_rep[:, 0:W], t.ones_row1[:, :], rs,
                    start=True, stop=True),
                 reads=[(t.row_rs.name, sl), t.ones_row1], writes=[t.ps_rep])
            P.op("DVE", lambda h, rsw=rsw: h.tensor_copy(
                    rsw[:, 0:W], t.ps_rep[:, 0:W]),
                 reads=[t.ps_rep], writes=[rsw])
            P.op("PE", lambda h, murs=murs: h.matmul(
                    t.ps_rep[:, 0:W], t.ones_row1[:, :], murs,
                    start=True, stop=True),
                 reads=[(t.row_murs.name, sl), t.ones_row1], writes=[t.ps_rep])
            P.op("DVE", lambda h, mursw=mursw: h.tensor_copy(
                    mursw[:, 0:W], t.ps_rep[:, 0:W]),
                 reads=[t.ps_rep], writes=[mursw])
            xf = xfw[sl]
            xf3 = xf[:, :].rearrange("p (c t) -> p c t", c=16)
            P.dma("qSP", xf3, src3[:, :, w0:w0 + W],
                  ch=f"xfw{sl}", writes=[xf])
            hsw = hsww[sl]
            hs3w = hsw[:, :].rearrange("p (c t) -> p c t", c=16)
            for dc in range(16):
                P.op("DVE", lambda h, dc=dc, xf3=xf3, rsw=rsw,
                     hs3w=hs3w: h.tensor_mul(
                        hs3w[:, dc, :], xf3[:, dc, :], rsw[:, 0:W]),
                     reads=[(xf.name, dc), rsw], writes=[(hsw.name, f"p{dc}")])
                P.op("DVE", lambda h, dc=dc, hs3w=hs3w, mursw=mursw:
                     h.tensor_sub(hs3w[:, dc, :], hs3w[:, dc, :], mursw[:, 0:W]),
                     reads=[(hsw.name, f"p{dc}"), mursw], writes=[(hsw.name, dc)])
            P.dma("qACT", hs3[:, :, w0:w0 + W], hsw[:, :].rearrange(
                    "p (c t) -> p c t", c=16),
                  reads=[hsw], ch=f"hsw{sl}", writes=[(hs_dst.name, w)])
            if ab is not None:
                nc_ab = ab["ncol"]
                if w == 0:
                    w32l = bf.sbuf(f"{tag}_w32", [128, 16 * 128], F32)
                    ab["_w32l"] = w32l
                    P.dma("qSP", w32l[:, :].rearrange("p (c k) -> p c k", c=16),
                          ab["w32_dram"].ap().rearrange("(c p) k -> p c k", p=128),
                          ch="w32", writes=[w32l])
                w32l = ab["_w32l"]
                w32_3 = w32l[:, :].rearrange("p (c k) -> p c k", c=16)
                P.ops("PE", [lambda h, dc=dc, w32_3=w32_3, xf3=xf3: h.matmul(
                        t.ps_ab[0:nc_ab, 0:W], w32_3[:, dc, :],
                        xf3[:, dc, :], start=(dc == 0), stop=(dc == 15))
                        for dc in range(16)],
                      reads=[xf, w32l], writes=[t.ps_ab])
                abw = ab["dst"][0:nc_ab, w0:w0 + W]
                P.op("DVE", lambda h, abw=abw, nc_ab=nc_ab, rsw=rsw:
                     h.tensor_mul(abw, t.ps_ab[0:nc_ab, 0:W], rsw[0:nc_ab, 0:W]),
                     reads=[t.ps_ab, rsw], writes=[(ab["dst"].name, f"p{w}")])
                P.op("DVE", lambda h, abw=abw, nc_ab=nc_ab, mursw=mursw:
                     h.scalar_tensor_tensor(
                        abw, mursw[0:nc_ab, 0:W], ab["colsum_col"][0:nc_ab, 0:1],
                        abw, ALU.mult, ALU.add),
                     reads=[(ab["dst"].name, f"p{w}"), mursw,
                            ab["colsum_col"]],
                     writes=[(ab["dst"].name, w)])


def emit_proj(P, nc, cfg, t, tag, hs_src, w_list, bias_sb=None):
    T, D = cfg.T, cfg.D
    NQ = max(1, T // 512)
    Wq = min(512, T)
    hs3 = hs_src.ap().rearrange("(c p) t -> p c t", p=128)
    with P.block(f"proj_{tag}"), P.buffers() as bf:
        hsr = bf.sbuf(f"{tag}_hsr", [128, 16 * T], BF16)
        hsr3 = hsr[:, :].rearrange("p (c t) -> p c t", c=16)
        for g in range(4):
            P.dma("qSP", hsr3[:, g * 4:(g + 1) * 4, :], hs3[:, g * 4:(g + 1) * 4, :],
                  ch=f"hsr{g % 2}", writes=[(hsr.name, g)])
        wt = [bf.sbuf(f"{tag}_wt{i}", [128, 16 * 128], BF16) for i in range(2)]
        st = [bf.sbuf(f"{tag}_st{i}", [128, 512], BF16) for i in range(2)]
        gct = 0
        for w_dram, out_dram in w_list:
            ncols = w_dram.shape[1]
            w3 = w_dram.ap().rearrange("(c p) k -> p c k", p=128)
            o3 = out_dram.ap().rearrange("(c p) t -> p c t", p=128)
            for ct in range(ncols // 128):
                wb = wt[gct % 2]
                wb3 = wb[:, :].rearrange("p (c k) -> p c k", c=16)
                P.dma("qPOOL", wb3, w3[:, :, ct * 128:(ct + 1) * 128],
                      ch=f"w{gct % 2}", writes=[wb])
                for q in range(NQ):
                    pbk = t.pb[q % 2]
                    P.ops("PE", [lambda h, dc=dc, q=q, wb3=wb3, pbk=pbk:
                                 h.matmul(pbk[:, 0:Wq],
                                          wb3[:, dc, :],
                                          hsr3[:, dc, q * Wq:(q + 1) * Wq],
                                          start=(dc == 0), stop=(dc == 15))
                                 for dc in range(16)],
                          reads=[wb] + [(hsr.name, g) for g in range(4)],
                          writes=[pbk])
                    stg = st[q % 2]
                    if bias_sb is None:
                        P.op("DVE", lambda h, stg=stg, pbk=pbk: h.tensor_copy(
                                stg[:, 0:Wq], pbk[:, 0:Wq]),
                             reads=[pbk], writes=[stg])
                    else:
                        P.op("DVE", lambda h, stg=stg, pbk=pbk, gct=gct:
                             h.tensor_scalar_add(
                                stg[:, 0:Wq], pbk[:, 0:Wq],
                                bias_sb[:, gct:gct + 1]),
                             reads=[pbk, bias_sb], writes=[stg])
                    P.dma("qACT", o3[:, ct, q * Wq:(q + 1) * Wq], stg[:, 0:Wq],
                          reads=[stg], ch=f"o{q % 2}",
                          writes=[(out_dram.name, ct)])
                gct += 1


def emit_conv_norm(P, nc, cfg, t):
    T = cfg.T
    NQ = max(1, T // 512)
    Wq = min(512, T)
    qkv3 = t.qkvT.ap().rearrange("(c p) t -> p c t", p=128)
    qkv23 = t.qkvT2.ap().rearrange("(c p) t -> p c t", p=128)
    with P.block("convnorm"), P.buffers() as bf:
        conv_sb = bf.sbuf("conv_sb", [128, 24 * 4], F32)
        cv3 = conv_sb[:, :].rearrange("p (c k) -> p c k", c=24)
        P.dma("qSP", cv3, t.conv_w.ap().rearrange("(c p) k -> p c k", p=128),
              ch="cw", writes=[conv_sb])
        xp = [bf.sbuf(f"cn_xp{i}", [128, T + 8], BF16) for i in range(2)]
        acc = [bf.sbuf(f"cn_acc{i}", [128, T], F32) for i in range(2)]
        sil = [bf.sbuf(f"cn_sil{i}", [128, T], BF16) for i in range(2)]
        sqb = [bf.sbuf(f"cn_sq{i}", [128, T], BF16) for i in range(2)]
        rrow = bf.sbuf("cn_rrow", [1, 2 * Wq], F32)
        rrep = [bf.sbuf(f"cn_rrep{i}", [128, Wq], F32) for i in range(2)]
        for u in range(24):
            s = u % 2
            xpb, accb, silb = xp[s], acc[s], sil[s]
            P.op("POOL", lambda h, xpb=xpb: h.memset(xpb[:, 0:8], 0.0),
                 writes=[(xpb.name, "pad")])
            P.dma("qSP", xpb[:, 8:8 + T], qkv3[:, u, :], ch=f"cnx{s}",
                  writes=[(xpb.name, "x")])
            P.op("DVE", lambda h, xpb=xpb, accb=accb, u=u: h.tensor_scalar_mul(
                    accb[:, :], xpb[:, 5:5 + T], cv3[:, u, 0:1]),
                 reads=[(xpb.name, "x"), (xpb.name, "pad"), conv_sb],
                 writes=[accb])
            for j in range(1, 4):
                P.op("DVE", lambda h, xpb=xpb, accb=accb, u=u, j=j:
                     h.scalar_tensor_tensor(
                        accb[:, :], xpb[:, 5 + j:5 + j + T], cv3[:, u, j:j + 1],
                        accb[:, :], ALU.mult, ALU.add),
                     reads=[(xpb.name, "x"), (xpb.name, "pad"), conv_sb, accb],
                     writes=[accb])
            P.op("ACT", lambda h, accb=accb, silb=silb: h.activation(
                    silb[:, :], accb[:, :], AF.Silu),
                 reads=[accb], writes=[silb])
            if u < 16:
                sqq = sqb[s]
                P.op("DVE", lambda h, silb=silb, sqq=sqq: h.tensor_mul(
                        sqq[:, :], silb[:, :], silb[:, :]),
                     reads=[silb], writes=[sqq])
                for q in range(NQ):
                    qs = q % 2
                    P.ops("PE", [lambda h, q=q, sqq=sqq: h.matmul(
                            t.ps_row_x[0:1, 0:Wq], t.ones_col[:, :],
                            sqq[:, q * Wq:(q + 1) * Wq], start=True, stop=True)],
                          reads=[sqq, t.ones_col], writes=[t.ps_row_x])
                    rr = rrow[0:1, qs * Wq:(qs + 1) * Wq]
                    P.op("ACT", lambda h, rr=rr: h.activation(
                            rr, t.ps_row_x[0:1, 0:Wq], AF.Ln,
                            bias=t.cst[0:1, 1:2]),
                         reads=[t.ps_row_x, (t.cst.name, 1)],
                         writes=[(rrow.name, f"l{qs}")])
                    bias_i = 4 if u < 8 else 5
                    P.op("ACT", lambda h, rr=rr, bias_i=bias_i: h.activation(
                            rr, rr, AF.Exp, scale=-0.5,
                            bias=t.cst[0:1, bias_i:bias_i + 1]),
                         reads=[(rrow.name, f"l{qs}"), (t.cst.name, bias_i)],
                         writes=[(rrow.name, qs)])
                    P.op("PE", lambda h, rr=rr: h.matmul(
                            t.ps_rep[:, 0:Wq], t.ones_row1[:, :], rr,
                            start=True, stop=True),
                         reads=[(rrow.name, qs), t.ones_row1], writes=[t.ps_rep])
                    rp = rrep[qs]
                    P.op("DVE", lambda h, rp=rp: h.tensor_copy(
                            rp[:, 0:Wq], t.ps_rep[:, 0:Wq]),
                         reads=[t.ps_rep], writes=[rp])
                    P.op("DVE", lambda h, silb=silb, rp=rp, q=q: h.tensor_mul(
                            silb[:, q * Wq:(q + 1) * Wq],
                            silb[:, q * Wq:(q + 1) * Wq], rp[:, 0:Wq]),
                         reads=[rp, silb], writes=[(silb.name, f"n{q}")])
            P.dma("qACT", qkv23[:, u, :], silb[:, :],
                  reads=[silb] + ([(silb.name, f"n{q}") for q in range(NQ)]
                                  if u < 16 else []),
                 ch=f"cno{s}", writes=[(t.qkvT2.name, u)])


def pbf(pb):
    return pb[:, 0:64].bitcast(BF16)


def emit_delta(P, nc, cfg, t):
    T = cfg.T
    NC = T // 128
    CG = min(8, NC)
    qkv3 = t.qkvT2.ap().rearrange("(c p) t -> p c t", p=128)
    gate3 = t.gateT.ap().rearrange("(c p) t -> p c t", p=128)
    oT3 = t.oT.ap().rearrange("(c p) t -> p c t", p=128)

    with P.block("delta"), P.buffers() as bf:
        GB8 = bf.sbuf("GB8", [64, T], F32)
        hv = bf.sbuf("hv", [16, 2], F32)
        nA = bf.sbuf("nA", [16, 1], F32)
        ones_blk = bf.sbuf("ones_blk", [16, 128], F32)
        tpg = bf.sbuf("tpg", [16, T], F32)
        onw_rep = bf.sbuf("onw_rep", [128, 128], F32)
        onw_row = bf.sbuf("onw_row", [1, 128], F32)

        gbc = [bf.sbuf(f"gbc{i}", [128, 64], F32) for i in range(2)]
        S = [bf.sbuf(f"S{i}", [128, 128], F32) for i in range(8)]
        Sb = [bf.sbuf(f"Sb{i}", [128, 128], BF16) for i in range(8)]
        qb = [bf.sbuf(f"qb{i}", [128, CG * 128], BF16) for i in range(8)]
        kb = [bf.sbuf(f"kb{i}", [128, CG * 128], BF16) for i in range(8)]
        vb = [bf.sbuf(f"vb{i}", [128, CG * 128], BF16) for i in range(8)]
        gb = [bf.sbuf(f"gb{i}", [128, CG * 128], BF16) for i in range(8)]
        ost = [bf.sbuf(f"ost{i}", [128, CG * 128], BF16) for i in range(8)]
        def slot_bufs(i):
            d = {}
            for n in ("Xs", "Es", "Ein", "bEn", "sgate", "om", "sq2", "on"):
                d[n] = bf.sbuf(f"sl{i}_{n}", [128, 128], F32)
            for n in ("Gam", "Cb", "CTb", "PTb", "qg", "vbb", "rb", "db",
                      "kdT", "on2", "P0", "P1", "Ck0", "CkT0", "Ck1", "CkT1"):
                d[n] = bf.sbuf(f"sl{i}_{n}", [128, 128], BF16)
            for n in ("gce", "ngb", "lam", "ms", "rstd"):
                d[n] = bf.sbuf(f"sl{i}_{n}", [128, 1], F32)
            return d
        slots = [slot_bufs(i) for i in range(2)]

        P.op("POOL", lambda h: h.memset(GB8[:, :], 0.0), writes=[GB8])
        P.dma("qSP", hv[:, :], t.head_vec[:, :], ch="hv")
        P.op("POOL", lambda h: h.memset(ones_blk[:, :], 1.0), writes=[ones_blk])
        P.op("ACT", lambda h: h.activation(nA[:, :], hv[:, 0:1], AF.Exp),
             reads=[hv], writes=[(nA.name, "e")])
        P.op("DVE", lambda h: h.tensor_scalar_mul(nA[:, :], nA[:, :], -1.0),
             reads=[(nA.name, "e")], writes=[nA])
        P.dma("qSP", onw_row[:, :], t.o_norm_w[:, :], ch="hv")
        P.op("PE", lambda h: h.matmul(t.ps_rep[:, 0:128], t.ones_row1[:, :],
                                      onw_row[0:1, :], start=True, stop=True),
             reads=[onw_row, t.ones_row1], writes=[t.ps_rep])
        P.op("DVE", lambda h: h.tensor_copy(onw_rep[:, :], t.ps_rep[:, 0:128]),
             reads=[t.ps_rep], writes=[onw_rep])

        P.op("DVE", lambda h: h.tensor_scalar_add(
                tpg[0:8, :], t.ab_sb[0:8, :], hv[0:8, 1:2]),
             reads=[(t.ab_sb.name, "*"), hv], writes=[(tpg.name, "a")])
        P.op("DVE", lambda h: h.tensor_scalar_max(
                GB8[0:8, :], tpg[0:8, :], 0.0),
             reads=[(tpg.name, "a")], writes=[(GB8.name, "g8")])
        P.op("ACT", lambda h: h.activation(
                tpg[0:8, :], tpg[0:8, :], AF.Abs),
             reads=[(tpg.name, "a")], writes=[(tpg.name, "a")])
        P.op("ACT", lambda h: h.activation(
                tpg[0:8, :], tpg[0:8, :], AF.Exp, scale=-1.0),
             reads=[(tpg.name, "a")], writes=[(tpg.name, "a")])
        P.op("ACT", lambda h: h.activation(
                tpg[0:8, :], tpg[0:8, :], AF.Ln, bias=t.cst[0:8, 3:4]),
             reads=[(tpg.name, "a"), (t.cst.name, 3)], writes=[(tpg.name, "a")])
        P.op("DVE", lambda h: h.tensor_add(
                GB8[0:8, :], GB8[0:8, :], tpg[0:8, :]),
             reads=[(GB8.name, "g8"), (tpg.name, "a")],
             writes=[(GB8.name, "g8")])
        P.op("DVE", lambda h: h.tensor_scalar_mul(
                GB8[0:8, :], GB8[0:8, :], nA[0:8, 0:1]),
             reads=[(GB8.name, "g8"), nA], writes=[(GB8.name, "g8")])
        P.op("ACT", lambda h: h.activation(
                GB8[32:40, :], t.ab_sb[32:40, :], AF.Sigmoid),
             reads=[(t.ab_sb.name, "*")], writes=[(GB8.name, "beta")])
        P.op("DVE", lambda h: h.tensor_scalar_mul(
                GB8[32:40, :], GB8[32:40, :], 2.0),
             reads=[(GB8.name, "beta")], writes=[(GB8.name, "beta")])
        for c in range(NC):
            P.op("DVE", lambda h, c=c: h.tensor_tensor_scan(
                    GB8[0:8, c * 128:(c + 1) * 128], ones_blk[0:8, :],
                    GB8[0:8, c * 128:(c + 1) * 128], 0.0, ALU.mult, ALU.add),
                 reads=[(GB8.name, "g8"), ones_blk],
                 writes=[(GB8.name, f"G{c}")])

        bank = [0]
        def nxt():
            bank[0] = (bank[0] + 1) % 8
            return t.pb[bank[0]]
        pball = t.pb

        for g in range(NC // CG):
            for bh in range(8):
                for (buf, base) in ((qb, 0), (kb, 8), (vb, 16)):
                    P.dma("qSP", buf[bh][:, :],
                          qkv3[:, base + bh, g * CG * 128:(g + 1) * CG * 128],
                          ch=f"dl{bh % 4}", writes=[buf[bh]])
                P.dma("qACT", gb[bh][:, :],
                      gate3[:, bh, g * CG * 128:(g + 1) * CG * 128],
                      ch=f"dg{bh % 2}", writes=[gb[bh]])
            for ci in range(CG):
                c = g * CG + ci
                cs = c * 128
                psh = nxt()
                P.op("PE", lambda h, cs=cs, psh=psh: h.transpose(
                        psh[:, 0:64], GB8[0:64, cs:cs + 128],
                        t.ident_f[0:64, 0:64]),
                     reads=[(GB8.name, f"G{cs // 128}"), (GB8.name, "beta"),
                            t.ident_f],
                     writes=[psh])
                gbcc = gbc[ci % 2]
                P.op("DVE", lambda h, psh=psh, gbcc=gbcc: h.tensor_copy(
                        gbcc[:, :], psh[:, 0:64]),
                     reads=[psh], writes=[gbcc])
                for bh in range(8):
                    sl = slots[bh % 2]
                    emit_delta_chunk(P, nc, cfg, t, sl, bh, c, ci, cs,
                                     qb[bh], kb[bh], vb[bh], gb[bh],
                                     GB8, gbcc, t.sel16_sb, S[bh], Sb[bh],
                                     ost[bh], onw_rep, nxt,
                                     g == 0 and ci == 0)
            for bh in range(8):
                P.dma("qACT", oT3[:, bh, g * CG * 128:(g + 1) * CG * 128],
                      ost[bh][:, :], reads=[ost[bh]], ch=f"do{bh % 2}",
                      writes=[(t.oT.name, (bh, g))])


def emit_delta_chunk(P, nc, cfg, t, sl, bh, c, ci, cs, qbh, kbh, vbh, gbh,
                     GB8, gbcc, sel_sb, S, Sb, ost, onw_rep, nxt, first):
    co = ci * 128
    qc = qbh[:, co:co + 128]
    kc = kbh[:, co:co + 128]
    vc = vbh[:, co:co + 128]
    gc = gbh[:, co:co + 128]
    Gcol = gbcc[:, bh:bh + 1]
    bcol = gbcc[:, 32 + bh:32 + bh + 1]
    gkey = (GB8.name, f"G{c}")
    bkey = (GB8.name, "beta")

    if first:
        P.op("POOL", lambda h: h.memset(S[:, :], 0.0), writes=[S])
        P.op("POOL", lambda h: h.memset(Sb[:, :], 0.0), writes=[Sb])

    pbA = nxt()
    P.op("PE", lambda h, pbA=pbA: h.matmul(
            pbA[:, 0:128], sel_sb[0:64, bh * 128:(bh + 1) * 128],
            GB8[0:64, cs:cs + 128], start=True, stop=True),
         reads=[sel_sb, gkey], writes=[pbA])
    pbB = nxt()
    P.op("PE", lambda h, pbB=pbB: h.matmul(
            pbB[:, 0:128], sel_sb[0:64, (8 + bh) * 128:(9 + bh) * 128],
            GB8[0:64, cs:cs + 128], start=True, stop=True),
         reads=[sel_sb, bkey], writes=[pbB])
    P.op("DVE", lambda h, pbA=pbA: h.scalar_tensor_tensor(
            sl["Xs"][:, :], pbA[:, 0:128], Gcol, t.mask_strict_sb[:, :],
            ALU.subtract, ALU.add),
         reads=[pbA, gbcc, t.mask_strict_sb], writes=[sl["Xs"]])
    P.op("ACT", lambda h: h.activation(sl["Es"][:, :], sl["Xs"][:, :], AF.Exp),
         reads=[sl["Xs"]], writes=[sl["Es"]])
    P.op("ACT", lambda h, pbA=pbA: h.activation(
            sl["Gam"][:, :], pbA[:, 0:128], AF.Exp),
         reads=[pbA], writes=[sl["Gam"]])
    P.op("DVE", lambda h: h.tensor_add(
            sl["Ein"][:, :], sl["Es"][:, :], t.ident_f[:, :]),
         reads=[sl["Es"], t.ident_f], writes=[sl["Ein"]])
    P.op("DVE", lambda h, pbB=pbB: h.scalar_tensor_tensor(
            sl["bEn"][:, :], pbB[:, 0:128], -1.0, sl["Es"][:, :],
            ALU.mult, ALU.mult),
         reads=[pbB, sl["Es"]], writes=[sl["bEn"]])
    pbC = nxt()
    P.op("PE", lambda h, pbC=pbC: h.matmul(
            pbC[:, 0:128], kc, kc, start=True, stop=True),
         reads=[kbh], writes=[pbC])
    pbD = nxt()
    P.op("PE", lambda h, pbD=pbD: h.matmul(
            pbD[:, 0:128], kc, qc, start=True, stop=True),
         reads=[kbh, qbh], writes=[pbD])
    P.op("DVE", lambda h, pbC=pbC: h.tensor_mul(
            sl["Cb"][:, :], pbC[:, 0:128], sl["bEn"][:, :]),
         reads=[pbC, sl["bEn"]], writes=[sl["Cb"]])
    P.op("DVE", lambda h, pbD=pbD: h.tensor_mul(
            sl["PTb"][:, :], pbD[:, 0:128], sl["Ein"][:, :]),
         reads=[pbD, sl["Ein"]], writes=[sl["PTb"]])
    pbE = nxt()
    P.op("PE", lambda h, pbE=pbE: h.transpose(
            pbf(pbE), sl["Cb"][:, :], t.ident_bf[:, :]),
         reads=[sl["Cb"], t.ident_bf], writes=[pbE])
    P.op("DVE", lambda h, pbE=pbE: h.tensor_copy(
            sl["CTb"][:, :], pbf(pbE)),
         reads=[pbE], writes=[sl["CTb"]])
    P.op("DVE", lambda h: h.tensor_add(
            sl["P0"][:, :], sl["Cb"][:, :], t.ident_bf[:, :]),
         reads=[sl["Cb"], t.ident_bf], writes=[sl["P0"]])
    Ccur, CTcur = sl["Cb"], sl["CTb"]
    Pcur = sl["P0"]
    Pother = sl["P1"]
    for k in range(1, 7):
        Cn = sl["Ck0"] if k % 2 == 1 else sl["Ck1"]
        CTn = sl["CkT0"] if k % 2 == 1 else sl["CkT1"]
        pb1 = nxt()
        P.op("PE", lambda h, pb1=pb1, CTcur=CTcur, Ccur=Ccur: h.matmul(
                pb1[:, 0:128], CTcur[:, :], Ccur[:, :], start=True, stop=True),
             reads=[CTcur, Ccur], writes=[pb1])
        pb2 = nxt()
        P.op("PE", lambda h, pb2=pb2, CTcur=CTcur, Ccur=Ccur: h.matmul(
                pb2[:, 0:128], Ccur[:, :], CTcur[:, :], start=True, stop=True),
             reads=[CTcur, Ccur], writes=[pb2])
        P.op("DVE", lambda h, pb1=pb1, Cn=Cn: h.tensor_copy(
                Cn[:, :], pb1[:, 0:128]),
             reads=[pb1], writes=[Cn])
        P.op("DVE", lambda h, pb2=pb2, CTn=CTn: h.tensor_copy(
                CTn[:, :], pb2[:, 0:128]),
             reads=[pb2], writes=[CTn])
        pb3 = nxt()
        P.op("PE", lambda h, pb3=pb3, CTn=CTn, Pcur=Pcur: h.matmul(
                pb3[:, 0:128], CTn[:, :], Pcur[:, :], start=True, stop=True),
             reads=[CTn, Pcur], writes=[pb3])
        P.op("DVE", lambda h, pb3=pb3, Pcur=Pcur, Pother=Pother: h.tensor_add(
                Pother[:, :], pb3[:, 0:128], Pcur[:, :]),
             reads=[pb3, Pcur], writes=[Pother])
        Ccur, CTcur = Cn, CTn
        Pcur, Pother = Pother, Pcur
    TTinv = Pcur
    pbV = nxt()
    P.op("PE", lambda h, pbV=pbV: h.transpose(
            pbf(pbV), vc, t.ident_bf[:, :]),
         reads=[vbh, t.ident_bf], writes=[pbV])
    P.op("DVE", lambda h, pbV=pbV: h.tensor_scalar_mul(
            sl["vbb"][:, :], pbf(pbV), bcol),
         reads=[pbV, gbcc], writes=[sl["vbb"]])
    pbR = nxt()
    P.op("PE", lambda h, pbR=pbR: h.matmul(
            pbR[:, 0:128], kc, Sb[:, :], start=True, stop=True),
         reads=[kbh, Sb], writes=[pbR])
    P.op("ACT", lambda h: h.activation(sl["gce"][:, :], Gcol, AF.Exp),
         reads=[gbcc], writes=[sl["gce"]])
    P.op("DVE", lambda h: h.scalar_tensor_tensor(
            sl["ngb"][:, :], sl["gce"][:, :], -1.0, bcol, ALU.mult, ALU.mult),
         reads=[sl["gce"], gbcc], writes=[sl["ngb"]])
    P.op("DVE", lambda h, pbR=pbR: h.scalar_tensor_tensor(
            sl["rb"][:, :], pbR[:, 0:128], sl["ngb"][:, 0:1], sl["vbb"][:, :],
            ALU.mult, ALU.add),
         reads=[pbR, sl["ngb"], sl["vbb"]], writes=[sl["rb"]])
    pbDl = nxt()
    P.op("PE", lambda h, pbDl=pbDl, TTinv=TTinv: h.matmul(
            pbDl[:, 0:128], TTinv[:, :], sl["rb"][:, :], start=True, stop=True),
         reads=[TTinv, sl["rb"]], writes=[pbDl])
    P.op("DVE", lambda h, pbDl=pbDl: h.tensor_copy(
            sl["db"][:, :], pbDl[:, 0:128]),
         reads=[pbDl], writes=[sl["db"]])
    P.op("DVE", lambda h: h.tensor_mul(
            sl["qg"][:, :], qc, sl["Gam"][:, :]),
         reads=[qbh, sl["Gam"]], writes=[sl["qg"]])
    pbO = nxt()
    P.ops("PE", [
        lambda h, pbO=pbO: h.matmul(pbO[:, 0:128], sl["qg"][:, :], Sb[:, :],
                                    start=True, stop=False),
        lambda h, pbO=pbO: h.matmul(pbO[:, 0:128], sl["PTb"][:, :],
                                    sl["db"][:, :], start=False, stop=True),
    ], reads=[sl["qg"], Sb, sl["PTb"], sl["db"]], writes=[pbO])
    pbK = nxt()
    P.op("PE", lambda h, pbK=pbK: h.transpose(
            pbf(pbK), kc, t.ident_bf[:, :]),
         reads=[kbh, t.ident_bf], writes=[pbK])
    P.op("DVE", lambda h, pbK=pbK: h.tensor_scalar_mul(
            sl["kdT"][:, :], pbf(pbK),
            sl["Ein"][:, 127:128]),
         reads=[pbK, sl["Ein"]], writes=[sl["kdT"]])
    pbS = nxt()
    P.op("PE", lambda h, pbS=pbS: h.matmul(
            pbS[:, 0:128], sl["kdT"][:, :], sl["db"][:, :],
            start=True, stop=True),
         reads=[sl["kdT"], sl["db"]], writes=[pbS])
    P.op("DVE", lambda h: h.tensor_mul(
            sl["lam"][:, :], sl["Ein"][:, 127:128], sl["gce"][:, :]),
         reads=[sl["Ein"], sl["gce"]], writes=[sl["lam"]])
    P.op("DVE", lambda h, pbS=pbS: h.scalar_tensor_tensor(
            S[:, :], S[:, :], sl["lam"][:, 0:1], pbS[:, 0:128],
            ALU.mult, ALU.add),
         reads=[S, sl["lam"], pbS], writes=[S])
    P.op("DVE", lambda h: h.tensor_copy(Sb[:, :], S[:, :]),
         reads=[S], writes=[Sb])
    pbG = nxt()
    P.op("PE", lambda h, pbG=pbG: h.transpose(
            pbf(pbG), gc, t.ident_bf[:, :]),
         reads=[gbh, t.ident_bf], writes=[pbG])
    P.op("ACT", lambda h, pbG=pbG: h.activation(
            sl["sgate"][:, :], pbf(pbG), AF.Silu),
         reads=[pbG], writes=[sl["sgate"]])
    P.op("DVE", lambda h, pbO=pbO: h.tensor_mul(
            sl["om"][:, :], pbO[:, 0:128], sl["sgate"][:, :]),
         reads=[pbO, sl["sgate"]], writes=[sl["om"]])
    P.op("DVE", lambda h: h.tensor_mul(
            sl["sq2"][:, :], sl["om"][:, :], sl["om"][:, :]),
         reads=[sl["om"]], writes=[sl["sq2"]])
    P.op("DVE", lambda h: h.reduce_sum(
            sl["ms"][:, :], sl["sq2"][:, :], axis=AX.X),
         reads=[sl["sq2"]], writes=[sl["ms"]])
    P.op("ACT", lambda h: h.activation(
            sl["rstd"][:, :], sl["ms"][:, :], AF.Ln, scale=1.0 / 128,
            bias=t.cst[:, 2:3]),
         reads=[sl["ms"], (t.cst.name, 2)], writes=[(sl["rstd"].name, "l")])
    P.op("ACT", lambda h: h.activation(
            sl["rstd"][:, :], sl["rstd"][:, :], AF.Exp, scale=-0.5),
         reads=[(sl["rstd"].name, "l")], writes=[sl["rstd"]])
    P.op("DVE", lambda h: h.tensor_scalar_mul(
            sl["on"][:, :], sl["om"][:, :], sl["rstd"][:, 0:1]),
         reads=[sl["om"], sl["rstd"]], writes=[sl["on"]])
    P.op("DVE", lambda h: h.tensor_mul(
            sl["on2"][:, :], sl["on"][:, :], onw_rep[:, :]),
         reads=[sl["on"], onw_rep], writes=[sl["on2"]])
    pbN = nxt()
    P.op("PE", lambda h, pbN=pbN: h.transpose(
            pbf(pbN), sl["on2"][:, :], t.ident_bf[:, :]),
         reads=[sl["on2"], t.ident_bf], writes=[pbN])
    P.op("DVE", lambda h, pbN=pbN, co=co: h.tensor_copy(
            ost[:, co:co + 128], pbf(pbN)),
         reads=[pbN], writes=[(ost.name, co)])


def emit_out_partial(P, nc, cfg, t, tag, w_dram, oT_dram, bias_dram=None):
    T, D = cfg.T, cfg.D
    NQ = max(1, T // 512)
    Wq = min(512, T)
    o3 = oT_dram.ap().rearrange("(c p) t -> p c t", p=128)
    with P.block(f"wo_{tag}"), P.buffers() as bf:
        w_sb = bf.sbuf(f"{tag}_wsb", [128, 8 * D], BF16)
        w3 = w_sb[:, :].rearrange("p (c k) -> p c k", c=8)
        P.dma("qPOOL", w3, w_dram.ap().rearrange("(c p) k -> p c k", p=128),
              ch="wo", writes=[w_sb])
        ov = [bf.sbuf(f"{tag}_ov{i}", [128, T], BF16) for i in range(8)]
        for vc in range(8):
            P.dma("qSP", ov[vc][:, :], o3[:, vc, :], ch=f"ov{vc % 2}",
                  writes=[ov[vc]])
        stg = [bf.sbuf(f"{tag}_st{i}", [128, 512], F32) for i in range(2)]
        if bias_dram is not None:
            bias_sb = bf.sbuf(f"{tag}_bias", [128, 16], F32)
            P.dma("qSP", bias_sb[:, :], bias_dram[:, :],
                  ch="wo2", writes=[bias_sb])
        for dt in range(16):
            for w in range(NQ):
                pbk = t.pb[(dt * NQ + w) % 2]
                P.ops("PE", [lambda h, vc=vc, dt=dt, w=w, pbk=pbk: h.matmul(
                        pbk[:, 0:Wq], w3[:, vc, dt * 128:(dt + 1) * 128],
                        ov[vc][:, w * Wq:(w + 1) * Wq],
                        start=(vc == 0), stop=(vc == 7)) for vc in range(8)],
                      reads=[w_sb] + [ov[vc] for vc in range(8)], writes=[pbk])
                sg = stg[w % 2]
                if bias_dram is not None:
                    P.op("DVE", lambda h, sg=sg, pbk=pbk, dt=dt:
                         h.tensor_scalar_add(sg[:, 0:Wq], pbk[:, 0:Wq],
                                             bias_sb[:, dt:dt + 1]),
                         reads=[pbk, bias_sb], writes=[sg])
                else:
                    P.op("DVE", lambda h, sg=sg, pbk=pbk: h.tensor_copy(
                            sg[:, 0:Wq], pbk[:, 0:Wq]),
                         reads=[pbk], writes=[sg])
                q = (w * Wq) // cfg.ARCH
                P.dma("qACT", t.ar_in[q, dt * 128:(dt + 1) * 128,
                                      (w * Wq) % cfg.ARCH:
                                      (w * Wq) % cfg.ARCH + Wq],
                      sg[:, 0:Wq], reads=[sg], ch=f"arw{w % 2}",
                      writes=[(t.ar_in.name, (q, dt, w))])


def emit_allreduce(P, nc, cfg, t, tag):
    with P.block(f"ar_{tag}"):
        for q in range(cfg.NAR):
            P.cc(lambda h, q=q: h.collective_compute(
                    "AllReduce", ALU.add,
                    replica_groups=[[0, 1], [2, 3], [4, 5], [6, 7]],
                    ins=[t.ar_in[q].opt()], outs=[t.ar_out[q].opt()]),
                 reads=[(t.ar_in.name, "*")], writes=[(t.ar_out.name, q)])


def emit_add(P, nc, cfg, t, src, dst, tag):
    T = cfg.T
    with P.block(f"add_{tag}"), P.buffers() as bf:
        xf = [bf.sbuf(f"{tag}_axf{i}", [128, T], F32) for i in range(2)]
        xg = [bf.sbuf(f"{tag}_axg{i}", [128, T], F32) for i in range(2)]
        for dc in range(16):
            xft, xgt = xf[dc % 2], xg[dc % 2]
            P.dma("qSP", xft[:, 0:T], src[dc * 128:(dc + 1) * 128, :],
                  ch=f"axf{dc % 2}", writes=[xft])
            for q in range(cfg.NAR):
                P.dma("qACT", xgt[:, q * cfg.ARCH:(q + 1) * cfg.ARCH],
                      t.ar_out[q, dc * 128:(dc + 1) * 128, :],
                      reads=[(t.ar_out.name, q)],
                      ch=f"axg{dc % 2}", writes=[(xgt.name, q)])
            P.op("DVE", lambda h, xft=xft, xgt=xgt: h.tensor_add(
                    xft[:, 0:T], xft[:, 0:T], xgt[:, 0:T]),
                 reads=[xft, xgt] + [(xgt.name, q) for q in range(cfg.NAR)],
                 writes=[xft])
            P.dma("qSP", dst[dc * 128:(dc + 1) * 128, :], xft[:, 0:T],
                  reads=[xft], ch=f"axf{dc % 2}", writes=[(dst.name, dc)])


def emit_router(P, nc, cfg, t, comb_sb, tag):
    T = cfg.T
    NC = T // 128
    with P.block(f"router_{tag}"), P.buffers() as bf:
        def buf(name, width=16):
            return bf.sbuf(f"{tag}_{name}", [128, NC * width], F32)
        lg, sc, ch, tmp, m6, eq, cs0, cs1 = (
            buf(n) for n in ("lg", "sc", "ch", "tmp", "m6", "eq", "cs0", "cs1"))
        p8, p8b = buf("p8", 8), buf("p8b", 8)
        p4, p4b, p4c = buf("p4", 4), buf("p4b", 4), buf("p4c", 4)
        c1, c1b, c1c = buf("c1", 1), buf("c1b", 1), buf("c1c", 1)
        ones16 = bf.sbuf(f"{tag}_ones16", [128, 16], F32)
        bias_rep = bf.sbuf(f"{tag}_brep", [128, 16], F32)
        tie_rep = bf.sbuf(f"{tag}_trep", [128, 16], F32)

        def v3(b, w=16):
            return b[:, :].rearrange("p (c e) -> p c e", c=NC)

        def v4(b, w, x):
            return b[:, :].rearrange("p (c a b) -> p c a b", c=NC, b=x)

        P.op("POOL", lambda h: h.memset(ones16[:, :], 1.0), writes=[ones16])
        for (rep, srcd) in ((bias_rep, t.router_bias), (tie_rep, t.tie_eps)):
            rowb = bf.sbuf(f"{tag}_row_{srcd.name}", [1, 16], F32)
            P.dma("qSP", rowb[:, :], srcd[:, :], ch="rt", writes=[rowb])
            P.op("PE", lambda h, rowb=rowb: h.matmul(
                    t.ps_rep[:, 0:16], t.ones_row1[:, :], rowb[0:1, :],
                    start=True, stop=True),
                 reads=[rowb, t.ones_row1], writes=[t.ps_rep])
            P.op("DVE", lambda h, rep=rep: h.tensor_copy(
                    rep[:, :], t.ps_rep[:, 0:16]),
                 reads=[t.ps_rep], writes=[rep])

        def D(build, reads, writes):
            P.op("DVE", build, reads=reads, writes=writes)

        for c in range(NC):
            psh = t.pb[c % 2]
            P.op("PE", lambda h, c=c, psh=psh: h.transpose(
                    psh[:, 0:16], t.ab_sb[64:80, c * 128:(c + 1) * 128],
                    t.ident_f[64:80, 64:80]),
                 reads=[(t.ab_sb.name, "*"), t.ident_f], writes=[psh])
            P.op("DVE", lambda h, c=c, psh=psh: h.tensor_copy(
                    lg[:, c * 16:(c + 1) * 16], psh[:, 0:16]),
                 reads=[psh], writes=[(lg.name, c)])
        lgall = [(lg.name, c) for c in range(NC)]
        P.op("ACT", lambda h: h.activation(sc[:, :], lg[:, :], AF.Sigmoid),
             reads=lgall, writes=[sc])
        D(lambda h: h.tensor_add(
                v3(ch), v3(sc),
                bias_rep[:, :].rearrange("p (o e) -> p o e", o=1)
                .broadcast_to([128, NC, 16])),
          [sc, bias_rep], [ch])
        def vg(b):
            return b[:, :].rearrange("p (c e) -> p c e", e=4)

        def vg1(b):
            return b[:, :].rearrange("p (c e) -> p c e", e=1)

        D(lambda h: h.tensor_reduce(vg1(p4), vg(ch), AX.X, ALU.max),
          [ch], [(p4.name, "m1")])
        D(lambda h: h.tensor_tensor(
                vg(eq), vg(ch), vg1(p4).broadcast_to([128, NC * 4, 4]),
                ALU.is_ge),
          [ch, (p4.name, "m1")], [(eq.name, "e1")])
        D(lambda h: h.scalar_tensor_tensor(
                vg(tmp), vg(eq), -1e30, vg(ch), ALU.mult, ALU.add),
          [(eq.name, "e1"), ch], [(tmp.name, "v2")])
        D(lambda h: h.tensor_reduce(vg1(p4b), vg(tmp), AX.X, ALU.max),
          [(tmp.name, "v2")], [(p4b.name, "m2")])
        D(lambda h: h.tensor_add(p4[:, :], p4[:, :], p4b[:, :]),
          [(p4.name, "m1"), (p4b.name, "m2")], [(p4.name, "gsc")])
        D(lambda h: h.tensor_reduce(v3(c1, 1), v3(p4, 4), AX.X, ALU.max),
          [(p4.name, "gsc")], [(c1.name, "g1")])
        D(lambda h: h.tensor_tensor(
                v3(p4c, 4), v3(p4, 4),
                v3(c1, 1).broadcast_to([128, NC, 4]), ALU.is_ge),
          [(p4.name, "gsc"), (c1.name, "g1")], [(p4c.name, "eg")])
        D(lambda h: h.scalar_tensor_tensor(
                v3(p4c, 4), v3(p4c, 4), -1e30, v3(p4, 4), ALU.mult, ALU.add),
          [(p4c.name, "eg"), (p4.name, "gsc")], [(p4c.name, "rest")])
        D(lambda h: h.tensor_reduce(v3(c1c, 1), v3(p4c, 4), AX.X, ALU.max),
          [(p4c.name, "rest")], [c1c])
        D(lambda h: h.tensor_tensor(
                v3(p4b, 4), v3(p4, 4),
                v3(c1c, 1).broadcast_to([128, NC, 4]), ALU.is_ge),
          [(p4.name, "gsc"), c1c], [(p4b.name, "gmask")])
        D(lambda h: h.tensor_copy(
                v4(tmp, 16, 4),
                p4b[:, :].rearrange("p (c g o) -> p c g o", c=NC, o=1)
                .broadcast_to([128, NC, 4, 4])),
          [(p4b.name, "gmask")], [(tmp.name, "smask")])
        D(lambda h: h.tensor_mul(tmp[:, :], tmp[:, :], ch[:, :]),
          [(tmp.name, "smask"), ch], [(tmp.name, "mc")])
        D(lambda h: h.tensor_add(
                v3(tmp), v3(tmp),
                tie_rep[:, :].rearrange("p (o e) -> p o e", o=1)
                .broadcast_to([128, NC, 16])),
          [(tmp.name, "mc"), tie_rep], [tmp])
        P.op("POOL", lambda h: h.memset(m6[:, :], 0.0), writes=[m6])
        for it in range(6):
            D(lambda h: h.tensor_reduce(v3(c1, 1), v3(tmp), AX.X, ALU.max),
              [tmp], [c1])
            D(lambda h: h.tensor_tensor(
                    v3(eq), v3(tmp),
                    v3(c1, 1).broadcast_to([128, NC, 16]), ALU.is_ge),
              [tmp, c1], [eq])
            for c in range(NC):
                D(lambda h, c=c: h.tensor_tensor_scan(
                        cs0[:, c * 16:(c + 1) * 16], ones16[:, :],
                        eq[:, c * 16:(c + 1) * 16], 0.0, ALU.mult, ALU.add),
                  [eq, ones16], [(cs0.name, c)])
            cur = cs0
            D(lambda h, cur=cur: h.tensor_single_scalar(
                    cur[:, :], cur[:, :], 1.0, ALU.is_le),
              [(cs0.name, c) for c in range(NC)], [cur])
            D(lambda h, cur=cur: h.tensor_mul(eq[:, :], eq[:, :], cur[:, :]),
              [eq, cur], [(eq.name, "sel")])
            D(lambda h: h.tensor_add(m6[:, :], m6[:, :], eq[:, :]),
              [m6, (eq.name, "sel")], [m6])
            D(lambda h: h.scalar_tensor_tensor(
                    tmp[:, :], eq[:, :], -1e30, tmp[:, :], ALU.mult, ALU.add),
              [(eq.name, "sel"), tmp], [tmp])
        D(lambda h: h.tensor_mul(m6[:, :], m6[:, :], sc[:, :]),
          [m6, sc], [(m6.name, "comb")])
        D(lambda h: h.tensor_reduce(v3(c1, 1), v3(m6), AX.X, ALU.add),
          [(m6.name, "comb")], [c1])
        D(lambda h: h.tensor_scalar_add(c1b[:, :], c1[:, :], 1e-20),
          [c1], [c1b])
        P.op("DVE", lambda h: h.reciprocal(c1c[:, :], c1b[:, :]),
             reads=[c1b], writes=[(c1c.name, "rc")])
        D(lambda h: h.tensor_scalar_mul(c1c[:, :], c1c[:, :], 2.5),
          [(c1c.name, "rc")], [c1c])
        D(lambda h: h.tensor_mul(
                v3(m6), v3(m6), v3(c1c, 1).broadcast_to([128, NC, 16])),
          [(m6.name, "comb"), c1c], [m6])
        P.op("POOL", lambda h: h.memset(comb_sb[:, :], 0.0), writes=[comb_sb])
        for c in range(NC):
            psh = t.pb[c % 2]
            P.op("PE", lambda h, c=c, psh=psh: h.transpose(
                    psh[0:16, 0:128], m6[:, c * 16:(c + 1) * 16],
                    t.ident_f[:, :]),
                 reads=[m6, t.ident_f], writes=[psh])
            P.op("DVE", lambda h, c=c, psh=psh: h.tensor_copy(
                    comb_sb[0:16, c * 128:(c + 1) * 128], psh[0:16, 0:128]),
                 reads=[psh], writes=[(comb_sb.name, c)])


def emit_ffn(P, nc, cfg, t, tag):
    T, D = cfg.T, cfg.D
    PT = cfg.ARCH
    NP = T // PT
    NTK = PT // 512 if PT >= 512 else 1
    TK = min(512, PT)
    hs3 = t.hsT.ap().rearrange("(c p) t -> p c t", p=128)
    groups = [(0, 1), (2, 3), (4, 5), (6, 7), (8,)]
    with P.block(f"ffn_{tag}"), P.buffers() as bf:
        hsq = bf.sbuf(f"{tag}_hsq", [128, 16 * PT], BF16)
        hsq3 = hsq[:, :].rearrange("p (c t) -> p c t", c=16)
        wA = [bf.sbuf(f"{tag}_wA{i}", [128, 2 * 16 * 128], BF16)
              for i in range(2)]
        inter = [bf.sbuf(f"{tag}_it{i}", [128, 11 * PT], BF16)
                 for i in range(2)]
        wB = [bf.sbuf(f"{tag}_wB{i}", [128, 11 * 1024], BF16)
              for i in range(2)]
        cmb = [bf.sbuf(f"{tag}_cmb{i}", [128, PT], F32) for i in range(2)]
        sgb = [bf.sbuf(f"{tag}_sg{i}", [128, 512], F32) for i in range(2)]
        stg = sgb
        for p in range(NP):
            p0 = p * PT
            P.dma("qSP", hsq3, hs3[:, :, p0:p0 + PT], ch="hsq",
                  writes=[hsq])
            for gi, grp in enumerate(groups):
                for si, e in enumerate(grp):
                    cme = cmb[si]
                    if e < 8:
                        for q in range(NTK):
                            pbk = t.pb[6]
                            P.op("PE", lambda h, e=e, q=q, pbk=pbk, p0=p0: h.matmul(
                                    pbk[:, 0:TK],
                                    t.sel16_sb[0:64, e * 128:(e + 1) * 128],
                                    t.comb_sb[0:64, p0 + q * TK:p0 + (q + 1) * TK],
                                    start=True, stop=True),
                                 reads=[t.sel16_sb, (t.comb_sb.name, "*")],
                                 writes=[pbk])
                            P.op("DVE", lambda h, q=q, cme=cme, pbk=pbk:
                                 h.tensor_copy(cme[:, q * TK:(q + 1) * TK],
                                               pbk[:, 0:TK]),
                                 reads=[pbk], writes=[(cme.name, q)])
                    itb = inter[si]
                    itb3 = itb[:, :].rearrange("p (c t) -> p c t", c=11)
                    for it in range(11):
                        wab = wA[it % 2]
                        wab4 = wab[:, :].rearrange(
                            "p (g c m) -> p g c m", g=2, c=16)
                        P.dma("qPOOL", wab4,
                              t.expA[e, :, it], ch=f"wa{it % 2}",
                              writes=[wab])
                        for q in range(NTK):
                            pbg = t.pb[0 if q % 2 == 0 else 2]
                            pbu = t.pb[1 if q % 2 == 0 else 3]
                            P.ops("PE", [lambda h, dc=dc, q=q, wab4=wab4,
                                         pbg=pbg: h.matmul(
                                    pbg[:, 0:TK], wab4[:, 0, dc, :],
                                    hsq3[:, dc, q * TK:(q + 1) * TK],
                                    start=(dc == 0), stop=(dc == 15))
                                    for dc in range(16)],
                                  reads=[wab, hsq], writes=[pbg])
                            P.ops("PE", [lambda h, dc=dc, q=q, wab4=wab4,
                                         pbu=pbu: h.matmul(
                                    pbu[:, 0:TK], wab4[:, 1, dc, :],
                                    hsq3[:, dc, q * TK:(q + 1) * TK],
                                    start=(dc == 0), stop=(dc == 15))
                                    for dc in range(16)],
                                  reads=[wab, hsq], writes=[pbu])
                            sg = sgb[q % 2]
                            ii = sg
                            P.op("ACT", lambda h, sg=sg, pbg=pbg: h.activation(
                                    sg[:, 0:TK], pbg[:, 0:TK], AF.Silu),
                                 reads=[pbg], writes=[sg])
                            P.op("DVE", lambda h, sg=sg, pbu=pbu:
                                 h.tensor_mul(sg[:, 0:TK], sg[:, 0:TK],
                                              pbu[:, 0:TK]),
                                 reads=[sg, pbu], writes=[sg])
                            if e < 8:
                                P.op("DVE", lambda h, ii=ii, cme=cme, q=q,
                                     it=it, itb3=itb3: h.tensor_mul(
                                        itb3[:, it, q * TK:(q + 1) * TK],
                                        ii[:, 0:TK],
                                        cme[:, q * TK:(q + 1) * TK]),
                                     reads=[ii, (cme.name, q)],
                                     writes=[(itb.name, (it, q))])
                            else:
                                P.op("DVE", lambda h, ii=ii, q=q, it=it,
                                     itb3=itb3: h.tensor_copy(
                                        itb3[:, it, q * TK:(q + 1) * TK],
                                        ii[:, 0:TK]),
                                     reads=[ii], writes=[(itb.name, (it, q))])
                for dhalf in range(2):
                    for si, e in enumerate(grp):
                        wbb = wB[si]
                        wbb3 = wbb[:, :].rearrange("p (c n) -> p c n", c=11)
                        P.dma("qPOOL", wbb3,
                              t.expB[e, :, :, dhalf * 1024:(dhalf + 1) * 1024],
                              ch=f"wb{si}", writes=[wbb])
                    for dt in range(8):
                        for q in range(NTK):
                            pbk = t.pb[4 + (dt * NTK + q) % 2]
                            mms = []
                            for si in range(len(grp)):
                                wbb3 = wB[si][:, :].rearrange(
                                    "p (c n) -> p c n", c=11)
                                itb3 = inter[si][:, :].rearrange(
                                    "p (c t) -> p c t", c=11)
                                for ic in range(11):
                                    first = (si == 0 and ic == 0)
                                    last = (si == len(grp) - 1 and ic == 10)
                                    mms.append(
                                        lambda h, si=si, ic=ic, dt=dt, q=q,
                                        wbb3=wbb3, itb3=itb3, first=first,
                                        last=last, pbk=pbk: h.matmul(
                                            pbk[:, 0:TK],
                                            wbb3[:, ic, dt * 128:(dt + 1) * 128],
                                            itb3[:, ic, q * TK:(q + 1) * TK],
                                            start=first, stop=last))
                            P.ops("PE", mms,
                                  reads=[wB[si] for si in range(len(grp))] +
                                        [inter[si] for si in range(len(grp))] +
                                        [(inter[si].name, (ic, q))
                                         for si in range(len(grp))
                                         for ic in range(11)],
                                  writes=[pbk])
                            sgt = stg[q % 2]
                            P.op("DVE", lambda h, sgt=sgt, pbk=pbk:
                                 h.tensor_copy(sgt[:, 0:TK], pbk[:, 0:TK]),
                                 reads=[pbk], writes=[sgt])
                            dr = dhalf * 8 + dt
                            P.dma("qPOOL",
                                  t.ar_in[p, dr * 128:(dr + 1) * 128,
                                          q * TK:(q + 1) * TK],
                                  sgt[:, 0:TK], reads=[sgt],
                                  ch=f"acc{(dt + q) % 2}",
                                  writes=[(t.ar_in.name, (p, dr, q))],
                                  **({} if gi == 0 else
                                     {"accum_op": ALU.add}))


def emit_attn_b(P, nc, cfg, t):
    T = cfg.T
    NQ = max(1, T // 512)
    Wq = min(512, T)
    scale = 1.0 / math.sqrt(128.0)
    qkv3 = t.qkvT.ap().rearrange("(c p) t -> p c t", p=128)
    oT3 = t.oT.ap().rearrange("(c p) t -> p c t", p=128)
    with P.block("attnb"), P.buffers() as bf:
        cmask = bf.sbuf("cmask", [128, 4 * Wq], F32)
        cm3 = cmask[:, :].rearrange("p (a w) -> p a w", a=4)
        P.dma("qSP", cm3,
              t.causal_mask.ap().rearrange("(a p) w -> p a w", p=128)
              [:, :, 0:Wq],
              ch="cm", writes=[cmask])
        qs = [bf.sbuf(f"ab_q{i}", [128, T], BF16) for i in range(2)]
        ks = [bf.sbuf(f"ab_k{i}", [128, T], BF16) for i in range(2)]
        vss = [bf.sbuf(f"ab_vs{i}", [128, T], BF16) for i in range(2)]
        vTs = [bf.sbuf(f"ab_vT{i}", [128, T], BF16) for i in range(2)]
        ps_ = [bf.sbuf(f"ab_p{i}", [128, Wq], BF16) for i in range(4)]
        sm_ = [bf.sbuf(f"ab_sm{i}", [128, Wq], F32) for i in range(2)]
        rr_ = [bf.sbuf(f"ab_rr{i}", [1, Wq], F32) for i in range(2)]
        rp_ = [bf.sbuf(f"ab_rp{i}", [128, Wq], F32) for i in range(2)]
        og_ = [bf.sbuf(f"ab_og{i}", [128, Wq], BF16) for i in range(2)]
        for bh in range(8):
            sl = bh % 2
            qb, kb, vs, vT = qs[sl], ks[sl], vss[sl], vTs[sl]
            P.dma("qSP", qb[:, :], qkv3[:, bh, :], ch=f"abq{sl}", writes=[qb])
            P.dma("qSP", kb[:, :], qkv3[:, 8 + bh, :], ch=f"abk{sl}",
                  writes=[kb])
            P.dma("qACT", vs[:, :], qkv3[:, 16 + bh, :], ch=f"abv{sl}",
                  writes=[vs])
            for kt in range(T // 128):
                pbv = t.pb[4 + sl]
                P.op("PE", lambda h, kt=kt, pbv=pbv, vs=vs: h.transpose(
                        pbf(pbv), vs[:, kt * 128:(kt + 1) * 128],
                        t.ident_bf[:, :]),
                     reads=[vs, t.ident_bf], writes=[pbv])
                P.op("DVE", lambda h, kt=kt, pbv=pbv, vT=vT: h.tensor_copy(
                        vT[:, kt * 128:(kt + 1) * 128], pbf(pbv)),
                     reads=[pbv], writes=[(vT.name, kt)])
            for qt in range(NQ):
                nk = (qt * Wq + Wq) // 128
                pbo = t.pb[0 + sl]
                pbr = t.pb[2 + sl]
                for kt in range(nk):
                    pbs = t.pb[6 + (kt % 2)]
                    P.op("PE", lambda h, kt=kt, qt=qt, pbs=pbs, kb=kb, qb=qb:
                         h.matmul(pbs[:, 0:Wq], kb[:, kt * 128:(kt + 1) * 128],
                                  qb[:, qt * Wq:(qt + 1) * Wq],
                                  start=True, stop=True),
                         reads=[kb, qb], writes=[pbs])
                    diag = kt >= (qt * Wq) // 128
                    pe = ps_[2 * sl + (kt % 2)]
                    if diag:
                        a = kt - (qt * Wq) // 128
                        smb = sm_[sl]
                        P.op("DVE", lambda h, pbs=pbs, smb=smb, a=a:
                             h.tensor_add(smb[:, 0:Wq], pbs[:, 0:Wq],
                                          cm3[:, a, 0:Wq]),
                             reads=[pbs, cmask], writes=[smb])
                        P.op("ACT", lambda h, smb=smb, pe=pe: h.activation(
                                pe[:, 0:Wq], smb[:, 0:Wq], AF.Exp,
                                scale=scale),
                             reads=[smb], writes=[pe])
                    else:
                        P.op("ACT", lambda h, pbs=pbs, pe=pe: h.activation(
                                pe[:, 0:Wq], pbs[:, 0:Wq], AF.Exp,
                                scale=scale),
                             reads=[pbs], writes=[pe])
                    P.op("PE", lambda h, pe=pe, pbr=pbr, kt=kt, nk=nk: h.matmul(
                            pbr[0:1, 0:Wq], t.ones_col[:, :], pe[:, 0:Wq],
                            start=(kt == 0), stop=(kt == nk - 1)),
                         reads=[pe, t.ones_col],
                         writes=[pbr] if kt == nk - 1 else [(pbr.name, kt)])
                    P.op("PE", lambda h, pe=pe, pbo=pbo, kt=kt, vT=vT, nk=nk:
                         h.matmul(pbo[:, 0:Wq],
                                  vT[:, kt * 128:(kt + 1) * 128],
                                  pe[:, 0:Wq],
                                  start=(kt == 0), stop=(kt == nk - 1)),
                         reads=[pe, vT],
                         writes=[pbo] if kt == nk - 1 else [(pbo.name, kt)])
                rr = rr_[sl]
                P.op("DVE", lambda h, rr=rr, pbr=pbr: h.reciprocal(
                        rr[0:1, 0:Wq], pbr[0:1, 0:Wq]),
                     reads=[pbr], writes=[rr])
                pbx = t.pb[6]
                P.op("PE", lambda h, rr=rr, pbx=pbx: h.matmul(
                        pbx[:, 0:Wq], t.ones_row1[:, :], rr[0:1, 0:Wq],
                        start=True, stop=True),
                     reads=[rr, t.ones_row1], writes=[pbx])
                rp = rp_[sl]
                P.op("DVE", lambda h, rp=rp, pbx=pbx: h.tensor_copy(
                        rp[:, 0:Wq], pbx[:, 0:Wq]),
                     reads=[pbx], writes=[rp])
                og = og_[sl]
                P.op("DVE", lambda h, og=og, pbo=pbo, rp=rp: h.tensor_mul(
                        og[:, 0:Wq], pbo[:, 0:Wq], rp[:, 0:Wq]),
                     reads=[pbo, rp], writes=[og])
                P.dma("qACT", oT3[:, bh, qt * Wq:(qt + 1) * Wq], og[:, 0:Wq],
                      reads=[og], ch=f"abo{sl}", writes=[(t.oT.name, (bh, qt))])


def emit_tap(P, nc, cfg, t, name, src, shape, dtype=F32):
    tap = nc.dram_tensor(f"tap_{name}", shape, dtype, kind="ExternalOutput")
    t.taps[name] = tap
    P.dma("qPOOL", tap.ap(), src, ch="tap")


def build(cfg=None, taps=(), n_phases=99):
    cfg = cfg or Cfg()
    nc = bass.Bass("TRN2", target_bir_lowering=False, num_devices=8)
    t = declare(nc, cfg, taps)
    with contextlib.ExitStack() as st:
        P = Emitter(nc, st)
        alloc_common(nc, st, cfg, t)
        with P.block("consts"):
            emit_consts(P, t)

        negcs = st.enter_context(nc.sbuf_tensor("negcs", [128, 1], F32))
        ones_colf = st.enter_context(nc.sbuf_tensor("ones_colf", [128, 1], F32))
        with P.block("prep_ab"), P.buffers() as bfp:
            w32f = bfp.sbuf("w32f", [128, 16 * 128], F32)
            P.op("POOL", lambda h: h.memset(ones_colf[:, :], 1.0),
                 writes=[ones_colf])
            w32f3 = w32f[:, :].rearrange("p (c k) -> p c k", c=16)
            P.dma("qSP", w32f3,
                  t.w_ab128.ap().rearrange("(c p) k -> p c k", p=128),
                  ch="w32", writes=[w32f])
            P.ops("PE", [lambda h, dc=dc: h.matmul(
                    t.ps_ab[:, 0:1], w32f3[:, dc, :], ones_colf[:, :],
                    start=(dc == 0), stop=(dc == 15)) for dc in range(16)],
                  reads=[w32f, ones_colf], writes=[t.ps_ab])
            P.op("DVE", lambda h: h.tensor_scalar_mul(
                    negcs[:, :], t.ps_ab[:, 0:1], -1.0),
                 reads=[t.ps_ab], writes=[negcs])

        ab_base = {"w32_dram": t.w_ab128, "colsum_col": negcs, "ncol": 128}

        with P.buffers() as bsA:
            t.ab_sb = bsA.sbuf("ab_sb_a", [128, cfg.T], F32)
            emit_ln(P, nc, cfg, t, t.xT, t.hsT, "ln1",
                    ab=dict(ab_base, dst=t.ab_sb))
            emit_proj(P, nc, cfg, t, "qkva", t.hsT,
                      [(t.w_qkv, t.qkvT), (t.w_gate, t.gateT)])
            emit_conv_norm(P, nc, cfg, t)
            emit_delta(P, nc, cfg, t)
            if "comb" in taps:
                t.comb_sb = bsA.sbuf("comb_sb_t", [64, cfg.T], F32)
                emit_router(P, nc, cfg, t, t.comb_sb, "rt_test")
                with P.block("tap_comb"):
                    tapc = nc.dram_tensor("tap_comb", [64, cfg.T], F32,
                                          kind="ExternalOutput")
                    t.taps["comb"] = tapc
                    P.dma("qPOOL", tapc.ap(), t.comb_sb[:, :], ch="tap")
            if "ab" in taps:
                with P.block("tap_ab"):
                    tapa = nc.dram_tensor("tap_ab", [128, cfg.T], F32,
                                          kind="ExternalOutput")
                    t.taps["ab"] = tapa
                    P.dma("qPOOL", tapa.ap(), t.ab_sb[:, :], ch="tap")
        emit_out_partial(P, nc, cfg, t, "woa", t.w_o, t.oT)
        emit_allreduce(P, nc, cfg, t, "a1")
        emit_add(P, nc, cfg, t, t.xT, t.xA, "x1")

        if n_phases >= 2:
            with P.buffers() as bsB:
                t.comb_sb = bsB.sbuf("comb_sb_1", [64, cfg.T], F32)
                with P.buffers() as bab:
                    t.ab_sb = bab.sbuf("ab_sb_1", [128, cfg.T], F32)
                    emit_ln(P, nc, cfg, t, t.xA, t.hsT, "ln2",
                            ab=dict(ab_base, dst=t.ab_sb))
                    emit_router(P, nc, cfg, t, t.comb_sb, "rt1")
                    if "ab2" in taps:
                        with P.block("tap_ab2"):
                            tapa2 = nc.dram_tensor("tap_ab2", [128, cfg.T],
                                                   F32, kind="ExternalOutput")
                            t.taps["ab2"] = tapa2
                            P.dma("qPOOL", tapa2.ap(), t.ab_sb[:, :], ch="tap")
                if "comb2" in taps:
                    with P.block("tap_comb2"):
                        tapc = nc.dram_tensor("tap_comb2", [64, cfg.T], F32,
                                              kind="ExternalOutput")
                        t.taps["comb2"] = tapc
                        P.dma("qPOOL", tapc.ap(), t.comb_sb[:, :], ch="tap")
                emit_ffn(P, nc, cfg, t, "ffn1")
            emit_allreduce(P, nc, cfg, t, "a2")
            if "hs2" in taps:
                with P.block("tap_hs2"):
                    emit_tap(P, nc, cfg, t, "hs2", t.hsT.ap(),
                             [cfg.D, cfg.T], BF16)
                    emit_tap(P, nc, cfg, t, "ar2", t.ar_out.ap(),
                             [cfg.NAR, cfg.D, cfg.ARCH], F32)
            emit_add(P, nc, cfg, t, t.xA, t.xB, "x2")

        if n_phases >= 3:
            emit_ln(P, nc, cfg, t, t.xB, t.hsT, "ln3")
            emit_proj(P, nc, cfg, t, "qkvb", t.hsT, [(t.w_attn, t.qkvT)],
                      bias_sb=t.b_attn_sb)
            emit_attn_b(P, nc, cfg, t)
            emit_out_partial(P, nc, cfg, t, "wob", t.w_proj, t.oT,
                             bias_dram=t.b_proj)
            emit_allreduce(P, nc, cfg, t, "a3")
            emit_add(P, nc, cfg, t, t.xB, t.xA, "x3")

        if n_phases >= 4:
            with P.buffers() as bsC:
                t.comb_sb = bsC.sbuf("comb_sb_2", [64, cfg.T], F32)
                with P.buffers() as bab:
                    t.ab_sb = bab.sbuf("ab_sb_2", [128, cfg.T], F32)
                    emit_ln(P, nc, cfg, t, t.xA, t.hsT, "ln4",
                            ab=dict(ab_base, dst=t.ab_sb))
                    emit_router(P, nc, cfg, t, t.comb_sb, "rt2")
                emit_ffn(P, nc, cfg, t, "ffn2")
            emit_allreduce(P, nc, cfg, t, "a4")
            emit_add(P, nc, cfg, t, t.xA, t.out_xT, "x4")

        with P.block("taps"):
            if "hs1" in taps:
                emit_tap(P, nc, cfg, t, "hs1", t.hsT.ap(), [cfg.D, cfg.T], BF16)
            if "qkvT" in taps:
                emit_tap(P, nc, cfg, t, "qkvT", t.qkvT.ap(), [3072, cfg.T], BF16)
                emit_tap(P, nc, cfg, t, "gateT", t.gateT.ap(), [1024, cfg.T], BF16)
            if "qkvT2" in taps:
                emit_tap(P, nc, cfg, t, "qkvT2", t.qkvT2.ap(), [3072, cfg.T], BF16)
            if "oT" in taps:
                emit_tap(P, nc, cfg, t, "oT", t.oT.ap(), [1024, cfg.T], BF16)
            if "x1" in taps:
                emit_tap(P, nc, cfg, t, "x1", t.xA.ap(), [cfg.D, cfg.T], F32)
            if "x2" in taps:
                emit_tap(P, nc, cfg, t, "x2", t.xB.ap(), [cfg.D, cfg.T], F32)
            if "x3" in taps:
                emit_tap(P, nc, cfg, t, "x3", t.xA.ap(), [cfg.D, cfg.T], F32)

        with P.block("fin"):
            P.wait_all(["SP", "POOL", "ACT"])
    return nc, t



def kernel(**inputs):
    cfg = Cfg()
    in_maps = stage_inputs(inputs, cfg)
    nc, t = build(cfg)
    res = run_bass_kernel_spmd(nc, in_maps, core_ids=list(range(8)))
    outs = []
    for b in range(4):
        outs.append(np.asarray(res.results[2 * b]["out_xT"], np.float32).T)
    return np.stack(outs).astype(np.float32)


# version 06
# speedup vs baseline: 1.0135x; 1.0135x over previous
import contextlib
import math
import numpy as np
import ml_dtypes
import concourse.bass as bass
import concourse.mybir as mybir
from concourse.bass_utils import run_bass_kernel_spmd

F32 = mybir.dt.float32
BF16 = mybir.dt.bfloat16
AF = mybir.ActivationFunctionType
ALU = mybir.AluOpType
AX = mybir.AxisListType


ENGINES = ("PE", "ACT", "DVE", "POOL", "SP")
_BLOCK_ATTR = {"PE": "tensor", "ACT": "scalar", "DVE": "vector",
               "POOL": "gpsimd", "SP": "sync"}
_QUEUE_ENG = {"qSP": "SP", "qACT": "ACT", "qPOOL": "POOL"}


def _key_of(x):
    if isinstance(x, bass.AP):
        return (x.tensor.name, "*")
    if isinstance(x, tuple) and len(x) == 2:
        return x
    if isinstance(x, str):
        return (x, "*")
    if hasattr(x, "name"):
        return (x.name, "*")
    raise TypeError(f"bad dep key {x!r}")


class Emitter:
    class Op:
        __slots__ = ("sem", "val", "clock")
        def __init__(self, sem, val, clock):
            self.sem, self.val, self.clock = sem, val, clock

    def __init__(self, nc, stack, n_dma_sems=80):
        self.nc = nc
        self.sem = {}
        for name in ENGINES + ("CC",):
            self.sem[name] = stack.enter_context(nc.semaphore("sem_" + name))
        self.cnt = {name: 0 for name in ENGINES + ("CC",)}
        self.hb = {e: {} for e in ENGINES}
        self.writes = {}
        self.reads = {}
        self._free_sems = [stack.enter_context(nc.semaphore(f"dsem_{i}"))
                           for i in range(n_dma_sems)]
        self._chan_last = {}
        self._rec = None
        self._freed = []
        self._seed = {}
        self.n_ops = 0
        self.n_waits = 0

    def _need(self, eng, dep, waits):
        if dep is None:
            return
        hb = self.hb[eng]
        if hb.get(dep.sem, 0) >= dep.val:
            return
        waits.append((dep.sem, dep.val))
        hb[dep.sem] = dep.val
        for s, v in dep.clock.items():
            if hb.get(s, 0) < v:
                hb[s] = v

    def _dep_ops(self, key, for_write):
        tname, sub = key
        wmap = self.writes.get(tname, {})
        rmap = self.reads.get(tname, {})
        deps = list(self._seed.get(tname, ()))
        if sub == "*":
            deps.extend(wmap.values())
            if for_write:
                for lst in rmap.values():
                    deps.extend(lst)
        else:
            if sub in wmap:
                deps.append(wmap[sub])
            if "*" in wmap:
                deps.append(wmap["*"])
            if for_write:
                deps.extend(rmap.get(sub, ()))
                deps.extend(rmap.get("*", ()))
        return deps

    def _note_read(self, key, op):
        tname, sub = key
        self.reads.setdefault(tname, {}).setdefault(sub, []).append(op)

    def _note_write(self, key, op):
        tname, sub = key
        wmap = self.writes.setdefault(tname, {})
        rmap = self.reads.setdefault(tname, {})
        if sub == "*":
            wmap.clear()
            rmap.clear()
        else:
            rmap.pop(sub, None)
        wmap[sub] = op

    @contextlib.contextmanager
    def block(self, name=None):
        assert self._rec is None
        self._rec = {e: [] for e in ENGINES}
        yield self
        rec = self._rec
        self._rec = None
        with self.nc.Block(name) as blk:
            for eng in ENGINES:
                items = rec[eng]
                if not items:
                    continue
                def replay(h, items=items):
                    for fn in items:
                        fn(h)
                getattr(blk, _BLOCK_ATTR[eng])(replay)

    def _record(self, eng, fn):
        assert self._rec is not None, "emit outside of block()"
        self._rec[eng].append(fn)

    def op(self, eng, build, reads=(), writes=(), sem=None, inc=1):
        return self.ops(eng, [build], reads=reads, writes=writes, sem=sem,
                        inc=inc)

    def ops(self, eng, builds, reads=(), writes=(), sem=None, inc=1):
        sem_name = sem or eng
        rkeys, wkeys, waits = [], [], []
        for x in reads:
            if isinstance(x, Emitter.Op):
                self._need(eng, x, waits)
            else:
                rkeys.append(_key_of(x))
        for x in writes:
            if isinstance(x, Emitter.Op):
                self._need(eng, x, waits)
            else:
                wkeys.append(_key_of(x))
        for k in rkeys:
            for dep in self._dep_ops(k, for_write=False):
                self._need(eng, dep, waits)
        for k in wkeys:
            for dep in self._dep_ops(k, for_write=True):
                self._need(eng, dep, waits)
        self.cnt[sem_name] += inc
        val = self.cnt[sem_name]
        clock = dict(self.hb[eng])
        clock[sem_name] = val
        me = Emitter.Op(sem_name, val, clock)
        for k in rkeys:
            self._note_read(k, me)
        for k in wkeys:
            self._note_write(k, me)
        sem_handles = self.sem
        def fn(h, waits=tuple(waits), builds=tuple(builds),
               sem_h=sem_handles[sem_name], inc=inc):
            for s, v in waits:
                h.wait_ge(sem_handles[s], v)
            inst = None
            for b in builds:
                inst = b(h)
                assert inst is not None, f"builder returned None: {b.__code__.co_filename}:{b.__code__.co_firstlineno}"
            inst.then_inc(sem_h, inc)
        self._record(eng, fn)
        self.n_ops += len(builds)
        self.n_waits += len(waits)
        return me

    @contextlib.contextmanager
    def buffers(self):
        scope = _BufScope(self)
        try:
            yield scope
        finally:
            scope.close()

    def _tensor_ops(self, name):
        ops = []
        for op in self.writes.get(name, {}).values():
            ops.append(op)
        for lst in self.reads.get(name, {}).values():
            ops.extend(lst)
        return ops

    def _chan_sem(self, name):
        if name not in self.sem:
            assert self._free_sems, "out of DMA channel semaphores"
            self.sem[name] = self._free_sems.pop()
            self.cnt[name] = 0
        return name

    def dma(self, queue, out, in_, reads=(), writes=(), ch=None, inc=16, **kw):
        eng = _QUEUE_ENG[queue]
        sem_name = self._chan_sem(f"{queue}.{ch or 'd'}")
        prev = self._chan_last.get(sem_name)
        rd = (list(reads) + [in_]) if not reads else list(reads)
        if prev is not None:
            rd = rd + [prev]
        me = self.op(
            eng,
            lambda h: h.dma_start(out=out, in_=in_, **kw),
            reads=rd,
            writes=(list(writes) + [out]) if not writes else writes,
            sem=sem_name, inc=inc)
        self._chan_last[sem_name] = me
        return me

    def dmat(self, queue, out, in_, reads=(), writes=(), ch=None):
        eng = _QUEUE_ENG[queue]
        assert eng in ("SP", "ACT")
        sem_name = self._chan_sem(f"{queue}.{ch or 'dt'}")
        prev = self._chan_last.get(sem_name)
        rd = (list(reads) + [in_]) if not reads else list(reads)
        if prev is not None:
            rd = rd + [prev]
        me = self.op(
            eng,
            lambda h: h.dma_start(out=out, in_=in_, transpose=True),
            reads=rd,
            writes=(list(writes) + [out]) if not writes else writes,
            sem=sem_name, inc=16)
        self._chan_last[sem_name] = me
        return me

    def cc(self, build, reads=(), writes=(), inc=1):
        prev = self._chan_last.get("CC")
        rd = list(reads) + ([prev] if prev is not None else [])
        me = self.op("POOL", build, reads=rd, writes=writes, sem="CC", inc=inc)
        self._chan_last["CC"] = me
        return me

    def wait_all(self, eng_list=None):
        for eng in (eng_list or ["SP"]):
            waits = []
            for tname, wmap in self.writes.items():
                for dep in wmap.values():
                    self._need(eng, dep, waits)
            def fn(h, waits=tuple(waits), sems=self.sem):
                for s, v in waits:
                    h.wait_ge(sems[s], v)
            self._record(eng, fn)
            self.n_waits += len(waits)



class _BufScope:
    def __init__(self, P):
        self.P = P
        self._items = []

    def sbuf(self, name, shape, dtype):
        P = self.P
        nc = P.nc
        lo0, hi0 = nc.sbuf_base, nc.sbuf_top
        ctx = nc.sbuf_tensor(name, shape, dtype)
        ten = ctx.__enter__()
        lo1, hi1 = nc.sbuf_base, nc.sbuf_top
        rng = (lo0, lo1) if lo1 != lo0 else (hi1, hi0)
        seeds = []
        for (flo, fhi), ops in P._freed:
            if flo < rng[1] and rng[0] < fhi:
                seeds.extend(ops)
        if seeds:
            P._seed[ten.name] = seeds
        self._items.append((ctx, ten.name, rng))
        return ten

    def close(self):
        P = self.P
        for ctx, name, rng in reversed(self._items):
            ops = P._tensor_ops(name)
            if ops:
                P._freed.append((rng, ops))
            P.writes.pop(name, None)
            P.reads.pop(name, None)
            P._seed.pop(name, None)
            ctx.__exit__(None, None, None)


class Cfg:
    def __init__(self, T=4096):
        self.T = T
        self.D = 2048
        self.HK = 16
        self.DK = 128
        self.NH = 8
        self.NE = 8
        self.E = 16
        self.I = 1408
        self.SI = 1408
        self.C = 128
        self.TH = T // 2
        self.NC = T // 128
        self.NQ = T // 512
        self.ARCH = min(1024, T)
        self.NAR = T // self.ARCH
        self.eps_ln = 1e-5
        self.eps_l2 = 1e-6
        self.eps_rms = 1e-5



def _f32c(a):
    return np.ascontiguousarray(a, dtype=np.float32)


def stage_inputs(inp, cfg):
    T, D = cfg.T, cfg.D
    x = np.asarray(inp["L0_attn_a_hidden_states"])
    W_qkv = np.asarray(inp["L0_attn_a_W_qkv"])
    W_ab = np.asarray(inp["L0_attn_a_W_ab"])
    conv_w = np.asarray(inp["L0_attn_a_conv_w"])
    A_log = np.asarray(inp["L0_attn_a_A_log"])
    dt_bias = np.asarray(inp["L0_attn_a_dt_bias"])
    o_norm_w = np.asarray(inp["L0_attn_a_o_norm_w"])
    W_o = np.asarray(inp["L0_attn_a_W_o"])
    w_attn = np.asarray(inp["L0_attn_b_w_attn"])
    b_attn = np.asarray(inp["L0_attn_b_b_attn"])
    w_proj = np.asarray(inp["L0_attn_b_w_proj"])
    b_proj = np.asarray(inp["L0_attn_b_b_proj"])
    gate_weight = np.asarray(inp["L0_ffn_gate_weight"])
    e_bias = np.asarray(inp["L0_ffn_e_score_correction_bias"])
    gate_proj = np.asarray(inp["L0_ffn_gate_proj"])
    up_proj = np.asarray(inp["L0_ffn_up_proj"])
    down_proj = np.asarray(inp["L0_ffn_down_proj"])
    sh_gate = np.asarray(inp["L0_ffn_shared_gate_w"])
    sh_up = np.asarray(inp["L0_ffn_shared_up_w"])
    sh_down = np.asarray(inp["L0_ffn_shared_down_w"])

    ident = np.eye(128, dtype=np.float32)
    tri = np.triu(np.ones((128, 128), np.float32), 1)
    mask_strict = _f32c((tri - 1.0) * 1e30)
    cm = np.zeros((4, 128, 512), np.float32)
    for a in range(4):
        for tk in range(128):
            cm[a, tk, : a * 128 + tk] = -1e30
    causal_mask = _f32c(cm.reshape(512, 512))

    def blobA(gp, up):
        g = gp.reshape(11, 128, 16, 128).transpose(3, 0, 2, 1)
        u = up.reshape(11, 128, 16, 128).transpose(3, 0, 2, 1)
        return np.stack([g, u], axis=2)

    def blobB(dp):
        return dp.reshape(2048, 11, 128).transpose(2, 1, 0)

    in_maps = []
    for c in range(8):
        b, r = c // 2, c % 2
        own_h = list(range(8 * r, 8 * r + 8))
        perm = own_h + [h for h in range(16) if h not in own_h]
        qcols = np.concatenate([np.arange(h * 128, h * 128 + 128) for h in own_h])
        d = {}
        d["xT"] = _f32c(x[b, :T].T)
        d["w_qkv"] = _f32c(np.concatenate(
            [W_qkv[:, qcols], W_qkv[:, 2048 + qcols], W_qkv[:, 4096 + qcols]],
            axis=1))
        d["conv_w"] = _f32c(np.concatenate(
            [conv_w[qcols], conv_w[2048 + qcols], conv_w[4096 + qcols]], axis=0))
        w128 = np.zeros((2048, 128), np.float32)
        w128[:, 0:8] = W_ab[:, own_h]
        w128[:, 32:40] = W_ab[:, [16 + h for h in own_h]]
        w128[:, 64:80] = gate_weight[perm].T
        d["w_ab128"] = _f32c(w128)
        d["head_vec"] = _f32c(np.stack(
            [np.concatenate([A_log[own_h], np.zeros(8, np.float32)]),
             np.concatenate([dt_bias[own_h], np.zeros(8, np.float32)])],
            axis=0).T)
        d["w_gate"] = _f32c(W_ab[:, 32 + qcols])
        d["o_norm_w"] = _f32c(o_norm_w.reshape(1, 128))
        d["w_o"] = _f32c(W_o[qcols])
        d["w_attn"] = _f32c(np.concatenate(
            [w_attn[:, qcols], w_attn[:, 2048 + qcols], w_attn[:, 4096 + qcols]],
            axis=1))
        d["b_attn"] = _f32c(np.concatenate(
            [b_attn[qcols], b_attn[2048 + qcols], b_attn[4096 + qcols]]
        ).reshape(24, 128).T)
        d["w_proj"] = _f32c(w_proj[qcols])
        d["b_proj"] = _f32c(b_proj.reshape(16, 128).T if r == 0
                            else np.zeros((128, 16), np.float32))
        d["router_bias"] = _f32c(e_bias[perm].reshape(1, 16))
        d["tie_eps"] = _f32c(np.array([-1e-30 * j for j in perm]).reshape(1, 16))
        eA = [blobA(gate_proj[e], up_proj[e]) for e in [perm[i] for i in range(8)]]
        eA.append(blobA(sh_gate[r * 1408:(r + 1) * 1408],
                        sh_up[r * 1408:(r + 1) * 1408]))
        d["expA"] = _f32c(np.stack(eA))
        eB = [blobB(down_proj[e]) for e in [perm[i] for i in range(8)]]
        eB.append(blobB(sh_down[:, r * 1408:(r + 1) * 1408]))
        d["expB"] = _f32c(np.stack(eB))
        d["ident"] = ident
        sel = np.zeros((64, 16 * 128), np.float32)
        for j in range(16):
            row = j if j < 8 else 32 + (j - 8)
            sel[row, j * 128:(j + 1) * 128] = 1.0
        d["sel16"] = sel
        d["mask_strict"] = mask_strict
        d["causal_mask"] = causal_mask
        in_maps.append(d)
    return in_maps



class B:
    pass


def declare(nc, cfg, taps):
    T, D = cfg.T, cfg.D
    t = B()
    t.taps = {}
    di = lambda n, s: nc.dram_tensor(n, s, F32, kind="ExternalInput")
    t.xT = di("xT", [D, T])
    t.w_qkv = di("w_qkv", [D, 3072])
    t.conv_w = di("conv_w", [3072, 4])
    t.w_ab128 = di("w_ab128", [D, 128])
    t.head_vec = di("head_vec", [16, 2])
    t.w_gate = di("w_gate", [D, 1024])
    t.o_norm_w = di("o_norm_w", [1, 128])
    t.w_o = di("w_o", [1024, D])
    t.w_attn = di("w_attn", [D, 3072])
    t.b_attn = di("b_attn", [128, 24])
    t.w_proj = di("w_proj", [1024, D])
    t.b_proj = di("b_proj", [128, 16])
    t.router_bias = di("router_bias", [1, 16])
    t.tie_eps = di("tie_eps", [1, 16])
    t.expA = di("expA", [9, 128, 11, 2, 16, 128])
    t.expB = di("expB", [9, 128, 11, 2048])
    t.ident = di("ident", [128, 128])
    t.sel16 = di("sel16", [64, 16 * 128])
    t.mask_strict = di("mask_strict", [128, 128])
    t.causal_mask = di("causal_mask", [512, 512])

    t.out_xT = nc.dram_tensor("out_xT", [D, T], F32, kind="ExternalOutput")

    ds = lambda n, s, dt=F32: nc.dram_tensor(n, s, dt)
    t.xA = ds("xA", [D, T])
    t.xB = ds("xB", [D, T])
    t.hsT = ds("hsT", [D, T], BF16)
    t.qkvT = ds("qkvT", [3072, T], BF16)
    t.qkvT2 = ds("qkvT2", [3072, T], BF16)
    t.gateT = ds("gateT", [1024, T], BF16)
    t.oT = ds("oT", [1024, T], BF16)
    t.ar_in = ds("ar_in", [cfg.NAR, D, cfg.ARCH])
    t.ar_out = ds("ar_out", [cfg.NAR, D, cfg.ARCH])
    t.rows_dram = ds("rows_dram", [64, T])
    return t


def alloc_common(nc, st, cfg, t):
    E = st.enter_context
    T = cfg.T
    t.cst = E(nc.sbuf_tensor("cst", [128, 8], F32))
    t.ones_col = E(nc.sbuf_tensor("ones_col", [128, 1], BF16))
    t.ones_row1 = E(nc.sbuf_tensor("ones_row1", [1, 128], F32))
    t.ones_row1b = E(nc.sbuf_tensor("ones_row1b", [1, 128], BF16))
    t.ident_bf = E(nc.sbuf_tensor("ident_bf", [128, 128], BF16))
    t.ident_f = E(nc.sbuf_tensor("ident_f", [128, 128], F32))
    t.mask_strict_sb = E(nc.sbuf_tensor("mask_strict_sb", [128, 128], F32))
    W = 512 if T >= 512 else T
    t.row_mu = E(nc.sbuf_tensor("row_mu", [1, 2 * W], F32))
    t.row_rs = E(nc.sbuf_tensor("row_rs", [1, 2 * W], F32))
    t.row_murs = E(nc.sbuf_tensor("row_murs", [1, 2 * W], F32))
    t.sel16_sb = E(nc.sbuf_tensor("sel16_sb", [64, 16 * 128], F32))
    t.b_attn_sb = E(nc.sbuf_tensor("b_attn_sb", [128, 24], F32))
    t.rs_repw = [E(nc.sbuf_tensor(f"rs_repw{i}", [128, W], F32)) for i in range(2)]
    t.murs_repw = [E(nc.sbuf_tensor(f"murs_repw{i}", [128, W], F32)) for i in range(2)]
    t.pb = [E(nc.psum_tensor(f"pb{i}", [128, 512], F32)) for i in range(8)]
    t.ps_row_x = t.pb[4]
    t.ps_row_s = t.pb[5]
    t.ps_rep = t.pb[6]
    t.ps_ab = t.pb[7]


def emit_consts(P, t):
    P.op("POOL", lambda h: h.memset(t.cst[:, 0:1], 1e-5), writes=[(t.cst.name, 0)])
    P.op("POOL", lambda h: h.memset(t.cst[:, 1:2], 1e-6), writes=[(t.cst.name, 1)])
    P.op("POOL", lambda h: h.memset(t.cst[:, 2:3], 1e-5), writes=[(t.cst.name, 2)])
    P.op("POOL", lambda h: h.memset(t.cst[:, 3:4], 1.0), writes=[(t.cst.name, 3)])
    P.op("POOL", lambda h: h.memset(t.cst[:, 4:5], -0.5 * math.log(128.0)),
         writes=[(t.cst.name, 4)])
    P.op("POOL", lambda h: h.memset(t.cst[:, 5:6], 0.0), writes=[(t.cst.name, 5)])
    P.op("POOL", lambda h: h.memset(t.ones_col[:, :], 1.0), writes=[t.ones_col])
    P.op("POOL", lambda h: h.memset(t.ones_row1[:, :], 1.0), writes=[t.ones_row1])
    P.op("POOL", lambda h: h.memset(t.ones_row1b[:, :], 1.0), writes=[t.ones_row1b])
    P.dma("qPOOL", t.ident_bf[:, :], t.ident[:, :], ch="c0")
    P.dma("qPOOL", t.sel16_sb[:, :], t.sel16[:, :], ch="c0")
    P.dma("qSP", t.b_attn_sb[:, :], t.b_attn[:, :], ch="c1")
    P.dma("qSP", t.ident_f[:, :], t.ident[:, :], ch="c1")
    P.dma("qSP", t.mask_strict_sb[:, :], t.mask_strict[:, :], ch="c1")


def emit_ln(P, nc, cfg, t, src, hs_dst, tag, ab=None):
    T, D = cfg.T, cfg.D
    W = 512 if T >= 512 else T
    NW = T // W
    src3 = src.ap().rearrange("(c p) t -> p c t", p=128)
    hs3 = hs_dst.ap().rearrange("(c p) t -> p c t", p=128)

    with P.block(f"ln_{tag}"), P.buffers() as bf:
        _xb = bf.sbuf(f"{tag}_xbw0", [128, 16 * W], BF16)
        xbw = [_xb, _xb]
        _sq = bf.sbuf(f"{tag}_sqw0", [128, 16 * W], BF16)
        sqw = [_sq, _sq]
        xfw = [bf.sbuf(f"{tag}_xfw{i}", [128, 16 * W], F32) for i in range(2)]
        hsww = [bf.sbuf(f"{tag}_hsw{i}", [128, 16 * W], BF16) for i in range(2)]
        for w in range(NW):
            w0 = w * W
            sl = w % 2
            xb = xbw[sl]
            xb3 = xb[:, :].rearrange("p (c t) -> p c t", c=16)
            P.dma("qPOOL", xb3, src3[:, :, w0:w0 + W],
                  ch=f"xbw{sl}", writes=[xb])
            sq = sqw[sl]
            sq3 = sq[:, :].rearrange("p (c t) -> p c t", c=16)
            P.op("DVE", lambda h, xb=xb, sq=sq: h.tensor_mul(
                    sq[:, :], xb[:, :], xb[:, :]), reads=[xb], writes=[sq])
            P.ops("PE", [lambda h, dc=dc, xb3=xb3: h.matmul(
                    t.ps_row_x[0:1, 0:W], t.ones_col[:, :], xb3[:, dc, :],
                    start=(dc == 0), stop=(dc == 15)) for dc in range(16)],
                  reads=[xb, t.ones_col], writes=[t.ps_row_x])
            P.ops("PE", [lambda h, dc=dc, sq3=sq3: h.matmul(
                    t.ps_row_s[0:1, 0:W], t.ones_col[:, :], sq3[:, dc, :],
                    start=(dc == 0), stop=(dc == 15)) for dc in range(16)],
                  reads=[sq, t.ones_col], writes=[t.ps_row_s])
            mu = t.row_mu[0:1, sl * W:sl * W + W]
            rs = t.row_rs[0:1, sl * W:sl * W + W]
            murs = t.row_murs[0:1, sl * W:sl * W + W]
            P.op("DVE", lambda h, mu=mu: h.tensor_scalar_mul(
                    mu, t.ps_row_x[0:1, 0:W], 1.0 / D),
                 reads=[t.ps_row_x], writes=[(t.row_mu.name, sl)])
            P.op("DVE", lambda h, mu=mu, murs=murs: h.tensor_mul(
                    murs, mu, mu),
                 reads=[(t.row_mu.name, sl)], writes=[(t.row_murs.name, f"t{sl}")])
            P.op("DVE", lambda h, rs=rs, murs=murs: h.scalar_tensor_tensor(
                    rs, t.ps_row_s[0:1, 0:W], 1.0 / D, murs,
                    ALU.mult, ALU.subtract),
                 reads=[t.ps_row_s, (t.row_murs.name, f"t{sl}")],
                 writes=[(t.row_rs.name, f"v{sl}")])
            P.op("ACT", lambda h, rs=rs: h.activation(
                    rs, rs, AF.Ln, bias=t.cst[0:1, 0:1]),
                 reads=[(t.row_rs.name, f"v{sl}"), (t.cst.name, 0)],
                 writes=[(t.row_rs.name, f"l{sl}")])
            P.op("ACT", lambda h, rs=rs: h.activation(
                    rs, rs, AF.Exp, scale=-0.5),
                 reads=[(t.row_rs.name, f"l{sl}")], writes=[(t.row_rs.name, sl)])
            P.op("DVE", lambda h, mu=mu, rs=rs, murs=murs: h.tensor_mul(
                    murs, mu, rs),
                 reads=[(t.row_mu.name, sl), (t.row_rs.name, sl)],
                 writes=[(t.row_murs.name, sl)])
            rsw = t.rs_repw[sl]
            mursw = t.murs_repw[sl]
            P.op("PE", lambda h, rs=rs: h.matmul(
                    t.ps_rep[:, 0:W], t.ones_row1[:, :], rs,
                    start=True, stop=True),
                 reads=[(t.row_rs.name, sl), t.ones_row1], writes=[t.ps_rep])
            P.op("DVE", lambda h, rsw=rsw: h.tensor_copy(
                    rsw[:, 0:W], t.ps_rep[:, 0:W]),
                 reads=[t.ps_rep], writes=[rsw])
            P.op("PE", lambda h, murs=murs: h.matmul(
                    t.ps_rep[:, 0:W], t.ones_row1[:, :], murs,
                    start=True, stop=True),
                 reads=[(t.row_murs.name, sl), t.ones_row1], writes=[t.ps_rep])
            P.op("DVE", lambda h, mursw=mursw: h.tensor_copy(
                    mursw[:, 0:W], t.ps_rep[:, 0:W]),
                 reads=[t.ps_rep], writes=[mursw])
            xf = xfw[sl]
            xf3 = xf[:, :].rearrange("p (c t) -> p c t", c=16)
            P.dma("qSP", xf3, src3[:, :, w0:w0 + W],
                  ch=f"xfw{sl}", writes=[xf])
            hsw = hsww[sl]
            hs3w = hsw[:, :].rearrange("p (c t) -> p c t", c=16)
            for dc in range(16):
                P.op("DVE", lambda h, dc=dc, xf3=xf3, rsw=rsw,
                     hs3w=hs3w: h.tensor_mul(
                        hs3w[:, dc, :], xf3[:, dc, :], rsw[:, 0:W]),
                     reads=[(xf.name, dc), rsw], writes=[(hsw.name, f"p{dc}")])
                P.op("DVE", lambda h, dc=dc, hs3w=hs3w, mursw=mursw:
                     h.tensor_sub(hs3w[:, dc, :], hs3w[:, dc, :], mursw[:, 0:W]),
                     reads=[(hsw.name, f"p{dc}"), mursw], writes=[(hsw.name, dc)])
            P.dma("qACT", hs3[:, :, w0:w0 + W], hsw[:, :].rearrange(
                    "p (c t) -> p c t", c=16),
                  reads=[hsw], ch=f"hsw{sl}", writes=[(hs_dst.name, w)])
            if ab is not None:
                nc_ab = ab["ncol"]
                if w == 0:
                    w32l = bf.sbuf(f"{tag}_w32", [128, 16 * 128], F32)
                    ab["_w32l"] = w32l
                    P.dma("qSP", w32l[:, :].rearrange("p (c k) -> p c k", c=16),
                          ab["w32_dram"].ap().rearrange("(c p) k -> p c k", p=128),
                          ch="w32", writes=[w32l])
                w32l = ab["_w32l"]
                w32_3 = w32l[:, :].rearrange("p (c k) -> p c k", c=16)
                P.ops("PE", [lambda h, dc=dc, w32_3=w32_3, xf3=xf3: h.matmul(
                        t.ps_ab[0:nc_ab, 0:W], w32_3[:, dc, :],
                        xf3[:, dc, :], start=(dc == 0), stop=(dc == 15))
                        for dc in range(16)],
                      reads=[xf, w32l], writes=[t.ps_ab])
                abw = ab["dst"][0:nc_ab, w0:w0 + W]
                P.op("DVE", lambda h, abw=abw, nc_ab=nc_ab, rsw=rsw:
                     h.tensor_mul(abw, t.ps_ab[0:nc_ab, 0:W], rsw[0:nc_ab, 0:W]),
                     reads=[t.ps_ab, rsw], writes=[(ab["dst"].name, f"p{w}")])
                P.op("DVE", lambda h, abw=abw, nc_ab=nc_ab, mursw=mursw:
                     h.scalar_tensor_tensor(
                        abw, mursw[0:nc_ab, 0:W], ab["colsum_col"][0:nc_ab, 0:1],
                        abw, ALU.mult, ALU.add),
                     reads=[(ab["dst"].name, f"p{w}"), mursw,
                            ab["colsum_col"]],
                     writes=[(ab["dst"].name, w)])


def emit_proj(P, nc, cfg, t, tag, hs_src, w_list, bias_sb=None):
    T, D = cfg.T, cfg.D
    NQ = max(1, T // 512)
    Wq = min(512, T)
    hs3 = hs_src.ap().rearrange("(c p) t -> p c t", p=128)
    with P.block(f"proj_{tag}"), P.buffers() as bf:
        hsr = bf.sbuf(f"{tag}_hsr", [128, 16 * T], BF16)
        hsr3 = hsr[:, :].rearrange("p (c t) -> p c t", c=16)
        for g in range(4):
            P.dma("qSP", hsr3[:, g * 4:(g + 1) * 4, :], hs3[:, g * 4:(g + 1) * 4, :],
                  ch=f"hsr{g % 2}", writes=[(hsr.name, g)])
        wt = [bf.sbuf(f"{tag}_wt{i}", [128, 16 * 128], BF16) for i in range(2)]
        st = [bf.sbuf(f"{tag}_st{i}", [128, 512], BF16) for i in range(2)]
        gct = 0
        for w_dram, out_dram in w_list:
            ncols = w_dram.shape[1]
            w3 = w_dram.ap().rearrange("(c p) k -> p c k", p=128)
            o3 = out_dram.ap().rearrange("(c p) t -> p c t", p=128)
            for ct in range(ncols // 128):
                wb = wt[gct % 2]
                wb3 = wb[:, :].rearrange("p (c k) -> p c k", c=16)
                P.dma("qPOOL", wb3, w3[:, :, ct * 128:(ct + 1) * 128],
                      ch=f"w{gct % 2}", writes=[wb])
                for q in range(NQ):
                    pbk = t.pb[q % 2]
                    P.ops("PE", [lambda h, dc=dc, q=q, wb3=wb3, pbk=pbk:
                                 h.matmul(pbk[:, 0:Wq],
                                          wb3[:, dc, :],
                                          hsr3[:, dc, q * Wq:(q + 1) * Wq],
                                          start=(dc == 0), stop=(dc == 15))
                                 for dc in range(16)],
                          reads=[wb] + [(hsr.name, g) for g in range(4)],
                          writes=[pbk])
                    stg = st[q % 2]
                    if bias_sb is None:
                        P.op("DVE", lambda h, stg=stg, pbk=pbk: h.tensor_copy(
                                stg[:, 0:Wq], pbk[:, 0:Wq]),
                             reads=[pbk], writes=[stg])
                    else:
                        P.op("DVE", lambda h, stg=stg, pbk=pbk, gct=gct:
                             h.tensor_scalar_add(
                                stg[:, 0:Wq], pbk[:, 0:Wq],
                                bias_sb[:, gct:gct + 1]),
                             reads=[pbk, bias_sb], writes=[stg])
                    P.dma("qACT", o3[:, ct, q * Wq:(q + 1) * Wq], stg[:, 0:Wq],
                          reads=[stg], ch=f"o{q % 2}",
                          writes=[(out_dram.name, ct)])
                gct += 1


def emit_conv_norm(P, nc, cfg, t):
    T = cfg.T
    NQ = max(1, T // 512)
    Wq = min(512, T)
    qkv3 = t.qkvT.ap().rearrange("(c p) t -> p c t", p=128)
    qkv23 = t.qkvT2.ap().rearrange("(c p) t -> p c t", p=128)
    with P.block("convnorm"), P.buffers() as bf:
        conv_sb = bf.sbuf("conv_sb", [128, 24 * 4], F32)
        cv3 = conv_sb[:, :].rearrange("p (c k) -> p c k", c=24)
        P.dma("qSP", cv3, t.conv_w.ap().rearrange("(c p) k -> p c k", p=128),
              ch="cw", writes=[conv_sb])
        xp = [bf.sbuf(f"cn_xp{i}", [128, T + 8], BF16) for i in range(2)]
        acc = [bf.sbuf(f"cn_acc{i}", [128, T], F32) for i in range(2)]
        sil = [bf.sbuf(f"cn_sil{i}", [128, T], BF16) for i in range(2)]
        sqb = [bf.sbuf(f"cn_sq{i}", [128, T], BF16) for i in range(2)]
        rrow = bf.sbuf("cn_rrow", [1, 2 * Wq], F32)
        rrep = [bf.sbuf(f"cn_rrep{i}", [128, Wq], F32) for i in range(2)]
        for u in range(24):
            s = u % 2
            xpb, accb, silb = xp[s], acc[s], sil[s]
            P.op("POOL", lambda h, xpb=xpb: h.memset(xpb[:, 0:8], 0.0),
                 writes=[(xpb.name, "pad")])
            P.dma("qSP", xpb[:, 8:8 + T], qkv3[:, u, :], ch=f"cnx{s}",
                  writes=[(xpb.name, "x")])
            P.op("DVE", lambda h, xpb=xpb, accb=accb, u=u: h.tensor_scalar_mul(
                    accb[:, :], xpb[:, 5:5 + T], cv3[:, u, 0:1]),
                 reads=[(xpb.name, "x"), (xpb.name, "pad"), conv_sb],
                 writes=[accb])
            for j in range(1, 4):
                P.op("DVE", lambda h, xpb=xpb, accb=accb, u=u, j=j:
                     h.scalar_tensor_tensor(
                        accb[:, :], xpb[:, 5 + j:5 + j + T], cv3[:, u, j:j + 1],
                        accb[:, :], ALU.mult, ALU.add),
                     reads=[(xpb.name, "x"), (xpb.name, "pad"), conv_sb, accb],
                     writes=[accb])
            P.op("ACT", lambda h, accb=accb, silb=silb: h.activation(
                    silb[:, :], accb[:, :], AF.Silu),
                 reads=[accb], writes=[silb])
            if u < 16:
                sqq = sqb[s]
                P.op("DVE", lambda h, silb=silb, sqq=sqq: h.tensor_mul(
                        sqq[:, :], silb[:, :], silb[:, :]),
                     reads=[silb], writes=[sqq])
                for q in range(NQ):
                    qs = q % 2
                    P.ops("PE", [lambda h, q=q, sqq=sqq: h.matmul(
                            t.ps_row_x[0:1, 0:Wq], t.ones_col[:, :],
                            sqq[:, q * Wq:(q + 1) * Wq], start=True, stop=True)],
                          reads=[sqq, t.ones_col], writes=[t.ps_row_x])
                    rr = rrow[0:1, qs * Wq:(qs + 1) * Wq]
                    P.op("ACT", lambda h, rr=rr: h.activation(
                            rr, t.ps_row_x[0:1, 0:Wq], AF.Ln,
                            bias=t.cst[0:1, 1:2]),
                         reads=[t.ps_row_x, (t.cst.name, 1)],
                         writes=[(rrow.name, f"l{qs}")])
                    bias_i = 4 if u < 8 else 5
                    P.op("ACT", lambda h, rr=rr, bias_i=bias_i: h.activation(
                            rr, rr, AF.Exp, scale=-0.5,
                            bias=t.cst[0:1, bias_i:bias_i + 1]),
                         reads=[(rrow.name, f"l{qs}"), (t.cst.name, bias_i)],
                         writes=[(rrow.name, qs)])
                    P.op("PE", lambda h, rr=rr: h.matmul(
                            t.ps_rep[:, 0:Wq], t.ones_row1[:, :], rr,
                            start=True, stop=True),
                         reads=[(rrow.name, qs), t.ones_row1], writes=[t.ps_rep])
                    rp = rrep[qs]
                    P.op("DVE", lambda h, rp=rp: h.tensor_copy(
                            rp[:, 0:Wq], t.ps_rep[:, 0:Wq]),
                         reads=[t.ps_rep], writes=[rp])
                    P.op("DVE", lambda h, silb=silb, rp=rp, q=q: h.tensor_mul(
                            silb[:, q * Wq:(q + 1) * Wq],
                            silb[:, q * Wq:(q + 1) * Wq], rp[:, 0:Wq]),
                         reads=[rp, silb], writes=[(silb.name, f"n{q}")])
            P.dma("qACT", qkv23[:, u, :], silb[:, :],
                  reads=[silb] + ([(silb.name, f"n{q}") for q in range(NQ)]
                                  if u < 16 else []),
                 ch=f"cno{s}", writes=[(t.qkvT2.name, u)])


def pbf(pb):
    return pb[:, 0:64].bitcast(BF16)


class VB:
    __slots__ = ("ten", "off")

    def __init__(self, ten, off):
        self.ten, self.off = ten, off

    @property
    def f128(self):
        return self.ten[:, self.off:self.off + 128]

    @property
    def f64(self):
        return self.ten[:, self.off:self.off + 64]

    @property
    def bf(self):
        return self.ten[:, self.off:self.off + 64].bitcast(BF16)

    @property
    def key(self):
        return (self.ten.name, self.off)


def emit_delta(P, nc, cfg, t):
    T = cfg.T
    NC = T // 128
    CG = min(8, NC)
    qkv3 = t.qkvT2.ap().rearrange("(c p) t -> p c t", p=128)
    gate3 = t.gateT.ap().rearrange("(c p) t -> p c t", p=128)
    oT3 = t.oT.ap().rearrange("(c p) t -> p c t", p=128)

    with P.block("delta"), P.buffers() as bf:
        GB8 = bf.sbuf("GB8", [64, T], F32)
        hv = bf.sbuf("hv", [16, 2], F32)
        nA = bf.sbuf("nA", [16, 1], F32)
        ones_blk = bf.sbuf("ones_blk", [16, 128], F32)
        tpg = bf.sbuf("tpg", [16, T], F32)
        onw_rep = bf.sbuf("onw_rep", [128, 128], F32)
        onw_row = bf.sbuf("onw_row", [1, 128], F32)

        gbc = [bf.sbuf(f"gbc{i}", [128, 64], F32) for i in range(2)]
        S = [bf.sbuf(f"S{i}", [128, 128], F32) for i in range(8)]
        Sb = [bf.sbuf(f"Sb{i}", [128, 128], BF16) for i in range(8)]
        qb = [bf.sbuf(f"qb{i}", [128, CG * 128], BF16) for i in range(8)]
        kb = [bf.sbuf(f"kb{i}", [128, CG * 128], BF16) for i in range(8)]
        vb = [bf.sbuf(f"vb{i}", [128, CG * 128], BF16) for i in range(8)]
        gb = [bf.sbuf(f"gb{i}", [128, CG * 128], BF16) for i in range(8)]
        ost = [bf.sbuf(f"ost{i}", [128, CG * 128], BF16) for i in range(8)]
        def slot_bufs(i):
            d = {}
            for n in ("Xs", "Es", "Ein", "bEn", "sgate", "om", "sq2", "on"):
                d[n] = bf.sbuf(f"sl{i}_{n}", [128, 128], F32)
            for n in ("Gam", "Cb", "CTb", "PTb", "qg", "vbb", "rb", "db",
                      "kdT", "on2", "P0", "P1", "Ck0", "CkT0", "Ck1", "CkT1"):
                d[n] = bf.sbuf(f"sl{i}_{n}", [128, 128], BF16)
            for n in ("gce", "ngb", "lam", "ms", "rstd"):
                d[n] = bf.sbuf(f"sl{i}_{n}", [128, 1], F32)
            return d
        slots = [slot_bufs(i) for i in range(4)]

        P.op("POOL", lambda h: h.memset(GB8[:, :], 0.0), writes=[GB8])
        P.dma("qSP", hv[:, :], t.head_vec[:, :], ch="hv")
        P.op("POOL", lambda h: h.memset(ones_blk[:, :], 1.0), writes=[ones_blk])
        P.op("ACT", lambda h: h.activation(nA[:, :], hv[:, 0:1], AF.Exp),
             reads=[hv], writes=[(nA.name, "e")])
        P.op("DVE", lambda h: h.tensor_scalar_mul(nA[:, :], nA[:, :], -1.0),
             reads=[(nA.name, "e")], writes=[nA])
        P.dma("qSP", onw_row[:, :], t.o_norm_w[:, :], ch="hv")
        P.op("PE", lambda h: h.matmul(t.ps_rep[:, 0:128], t.ones_row1[:, :],
                                      onw_row[0:1, :], start=True, stop=True),
             reads=[onw_row, t.ones_row1], writes=[t.ps_rep])
        P.op("DVE", lambda h: h.tensor_copy(onw_rep[:, :], t.ps_rep[:, 0:128]),
             reads=[t.ps_rep], writes=[onw_rep])

        P.op("DVE", lambda h: h.tensor_scalar_add(
                tpg[0:8, :], t.ab_sb[0:8, :], hv[0:8, 1:2]),
             reads=[(t.ab_sb.name, "*"), hv], writes=[(tpg.name, "a")])
        P.op("DVE", lambda h: h.tensor_scalar_max(
                GB8[0:8, :], tpg[0:8, :], 0.0),
             reads=[(tpg.name, "a")], writes=[(GB8.name, "g8")])
        P.op("ACT", lambda h: h.activation(
                tpg[0:8, :], tpg[0:8, :], AF.Abs),
             reads=[(tpg.name, "a")], writes=[(tpg.name, "a")])
        P.op("ACT", lambda h: h.activation(
                tpg[0:8, :], tpg[0:8, :], AF.Exp, scale=-1.0),
             reads=[(tpg.name, "a")], writes=[(tpg.name, "a")])
        P.op("ACT", lambda h: h.activation(
                tpg[0:8, :], tpg[0:8, :], AF.Ln, bias=t.cst[0:8, 3:4]),
             reads=[(tpg.name, "a"), (t.cst.name, 3)], writes=[(tpg.name, "a")])
        P.op("DVE", lambda h: h.tensor_add(
                GB8[0:8, :], GB8[0:8, :], tpg[0:8, :]),
             reads=[(GB8.name, "g8"), (tpg.name, "a")],
             writes=[(GB8.name, "g8")])
        P.op("DVE", lambda h: h.tensor_scalar_mul(
                GB8[0:8, :], GB8[0:8, :], nA[0:8, 0:1]),
             reads=[(GB8.name, "g8"), nA], writes=[(GB8.name, "g8")])
        P.op("ACT", lambda h: h.activation(
                GB8[32:40, :], t.ab_sb[32:40, :], AF.Sigmoid),
             reads=[(t.ab_sb.name, "*")], writes=[(GB8.name, "beta")])
        P.op("DVE", lambda h: h.tensor_scalar_mul(
                GB8[32:40, :], GB8[32:40, :], 2.0),
             reads=[(GB8.name, "beta")], writes=[(GB8.name, "beta")])
        for c in range(NC):
            P.op("DVE", lambda h, c=c: h.tensor_tensor_scan(
                    GB8[0:8, c * 128:(c + 1) * 128], ones_blk[0:8, :],
                    GB8[0:8, c * 128:(c + 1) * 128], 0.0, ALU.mult, ALU.add),
                 reads=[(GB8.name, "g8"), ones_blk],
                 writes=[(GB8.name, f"G{c}")])

        bank = [0]
        def nxt():
            bank[0] = (bank[0] + 1) % 32
            return VB(t.pb[bank[0] % 8], 128 * (bank[0] // 8))

        for g in range(NC // CG):
            for bh in range(8):
                for (buf, base) in ((qb, 0), (kb, 8), (vb, 16)):
                    P.dma("qSP", buf[bh][:, :],
                          qkv3[:, base + bh, g * CG * 128:(g + 1) * CG * 128],
                          ch=f"dl{bh % 4}", writes=[buf[bh]])
                P.dma("qACT", gb[bh][:, :],
                      gate3[:, bh, g * CG * 128:(g + 1) * CG * 128],
                      ch=f"dg{bh % 2}", writes=[gb[bh]])
            for ci in range(CG):
                c = g * CG + ci
                cs = c * 128
                psh = nxt()
                P.op("PE", lambda h, cs=cs, psh=psh: h.transpose(
                        psh.f64, GB8[0:64, cs:cs + 128],
                        t.ident_f[0:64, 0:64]),
                     reads=[(GB8.name, f"G{cs // 128}"), (GB8.name, "beta"),
                            t.ident_f],
                     writes=[psh.key])
                gbcc = gbc[ci % 2]
                P.op("DVE", lambda h, psh=psh, gbcc=gbcc: h.tensor_copy(
                        gbcc[:, :], psh.f64),
                     reads=[psh.key], writes=[gbcc])
                for bh in range(8):
                    sl = slots[bh % 4]
                    emit_delta_chunk(P, nc, cfg, t, sl, bh, c, ci, cs,
                                     qb[bh], kb[bh], vb[bh], gb[bh],
                                     GB8, gbcc, t.sel16_sb, S[bh], Sb[bh],
                                     ost[bh], onw_rep, nxt,
                                     g == 0 and ci == 0)
            for bh in range(8):
                P.dma("qACT", oT3[:, bh, g * CG * 128:(g + 1) * CG * 128],
                      ost[bh][:, :], reads=[ost[bh]], ch=f"do{bh % 2}",
                      writes=[(t.oT.name, (bh, g))])


def emit_delta_chunk(P, nc, cfg, t, sl, bh, c, ci, cs, qbh, kbh, vbh, gbh,
                     GB8, gbcc, sel_sb, S, Sb, ost, onw_rep, nxt, first):
    co = ci * 128
    qc = qbh[:, co:co + 128]
    kc = kbh[:, co:co + 128]
    vc = vbh[:, co:co + 128]
    gc = gbh[:, co:co + 128]
    Gcol = gbcc[:, bh:bh + 1]
    bcol = gbcc[:, 32 + bh:32 + bh + 1]
    gkey = (GB8.name, f"G{c}")
    bkey = (GB8.name, "beta")

    if first:
        P.op("POOL", lambda h: h.memset(S[:, :], 0.0), writes=[S])
        P.op("POOL", lambda h: h.memset(Sb[:, :], 0.0), writes=[Sb])

    pbA = nxt()
    P.op("PE", lambda h, pbA=pbA: h.matmul(
            pbA.f128, sel_sb[0:64, bh * 128:(bh + 1) * 128],
            GB8[0:64, cs:cs + 128], start=True, stop=True),
         reads=[sel_sb, gkey], writes=[pbA.key])
    pbB = nxt()
    P.op("PE", lambda h, pbB=pbB: h.matmul(
            pbB.f128, sel_sb[0:64, (8 + bh) * 128:(9 + bh) * 128],
            GB8[0:64, cs:cs + 128], start=True, stop=True),
         reads=[sel_sb, bkey], writes=[pbB.key])
    P.op("DVE", lambda h, pbA=pbA: h.scalar_tensor_tensor(
            sl["Xs"][:, :], pbA.f128, Gcol, t.mask_strict_sb[:, :],
            ALU.subtract, ALU.add),
         reads=[pbA.key, gbcc, t.mask_strict_sb], writes=[sl["Xs"]])
    P.op("ACT", lambda h: h.activation(sl["Es"][:, :], sl["Xs"][:, :], AF.Exp),
         reads=[sl["Xs"]], writes=[sl["Es"]])
    P.op("ACT", lambda h, pbA=pbA: h.activation(
            sl["Gam"][:, :], pbA.f128, AF.Exp),
         reads=[pbA.key], writes=[sl["Gam"]])
    P.op("DVE", lambda h: h.tensor_add(
            sl["Ein"][:, :], sl["Es"][:, :], t.ident_f[:, :]),
         reads=[sl["Es"], t.ident_f], writes=[sl["Ein"]])
    P.op("DVE", lambda h, pbB=pbB: h.scalar_tensor_tensor(
            sl["bEn"][:, :], pbB.f128, -1.0, sl["Es"][:, :],
            ALU.mult, ALU.mult),
         reads=[pbB.key, sl["Es"]], writes=[sl["bEn"]])
    pbC = nxt()
    P.op("PE", lambda h, pbC=pbC: h.matmul(
            pbC.f128, kc, kc, start=True, stop=True),
         reads=[kbh], writes=[pbC.key])
    pbD = nxt()
    P.op("PE", lambda h, pbD=pbD: h.matmul(
            pbD.f128, kc, qc, start=True, stop=True),
         reads=[kbh, qbh], writes=[pbD.key])
    P.op("DVE", lambda h, pbC=pbC: h.tensor_mul(
            sl["Cb"][:, :], pbC.f128, sl["bEn"][:, :]),
         reads=[pbC.key, sl["bEn"]], writes=[sl["Cb"]])
    P.op("DVE", lambda h, pbD=pbD: h.tensor_mul(
            sl["PTb"][:, :], pbD.f128, sl["Ein"][:, :]),
         reads=[pbD.key, sl["Ein"]], writes=[sl["PTb"]])
    pbE = nxt()
    P.op("PE", lambda h, pbE=pbE: h.transpose(
            pbE.bf, sl["Cb"][:, :], t.ident_bf[:, :]),
         reads=[sl["Cb"], t.ident_bf], writes=[pbE.key])
    P.op("DVE", lambda h, pbE=pbE: h.tensor_copy(
            sl["CTb"][:, :], pbE.bf),
         reads=[pbE.key], writes=[sl["CTb"]])
    P.op("DVE", lambda h: h.tensor_add(
            sl["P0"][:, :], sl["Cb"][:, :], t.ident_bf[:, :]),
         reads=[sl["Cb"], t.ident_bf], writes=[sl["P0"]])
    Ccur, CTcur = sl["Cb"], sl["CTb"]
    Pcur = sl["P0"]
    Pother = sl["P1"]
    for k in range(1, 7):
        Cn = sl["Ck0"] if k % 2 == 1 else sl["Ck1"]
        CTn = sl["CkT0"] if k % 2 == 1 else sl["CkT1"]
        pb1 = nxt()
        P.op("PE", lambda h, pb1=pb1, CTcur=CTcur, Ccur=Ccur: h.matmul(
                pb1.f128, CTcur[:, :], Ccur[:, :], start=True, stop=True),
             reads=[CTcur, Ccur], writes=[pb1.key])
        pb2 = nxt()
        P.op("PE", lambda h, pb2=pb2, CTcur=CTcur, Ccur=Ccur: h.matmul(
                pb2.f128, Ccur[:, :], CTcur[:, :], start=True, stop=True),
             reads=[CTcur, Ccur], writes=[pb2.key])
        P.op("DVE", lambda h, pb1=pb1, Cn=Cn: h.tensor_copy(
                Cn[:, :], pb1.f128),
             reads=[pb1.key], writes=[Cn])
        P.op("DVE", lambda h, pb2=pb2, CTn=CTn: h.tensor_copy(
                CTn[:, :], pb2.f128),
             reads=[pb2.key], writes=[CTn])
        pb3 = nxt()
        P.op("PE", lambda h, pb3=pb3, CTn=CTn, Pcur=Pcur: h.matmul(
                pb3.f128, CTn[:, :], Pcur[:, :], start=True, stop=True),
             reads=[CTn, Pcur], writes=[pb3.key])
        P.op("DVE", lambda h, pb3=pb3, Pcur=Pcur, Pother=Pother: h.tensor_add(
                Pother[:, :], pb3.f128, Pcur[:, :]),
             reads=[pb3.key, Pcur], writes=[Pother])
        Ccur, CTcur = Cn, CTn
        Pcur, Pother = Pother, Pcur
    TTinv = Pcur
    pbV = nxt()
    P.op("PE", lambda h, pbV=pbV: h.transpose(
            pbV.bf, vc, t.ident_bf[:, :]),
         reads=[vbh, t.ident_bf], writes=[pbV.key])
    P.op("DVE", lambda h, pbV=pbV: h.tensor_scalar_mul(
            sl["vbb"][:, :], pbV.bf, bcol),
         reads=[pbV.key, gbcc], writes=[sl["vbb"]])
    pbR = nxt()
    P.op("PE", lambda h, pbR=pbR: h.matmul(
            pbR.f128, kc, Sb[:, :], start=True, stop=True),
         reads=[kbh, Sb], writes=[pbR.key])
    P.op("ACT", lambda h: h.activation(sl["gce"][:, :], Gcol, AF.Exp),
         reads=[gbcc], writes=[sl["gce"]])
    P.op("DVE", lambda h: h.scalar_tensor_tensor(
            sl["ngb"][:, :], sl["gce"][:, :], -1.0, bcol, ALU.mult, ALU.mult),
         reads=[sl["gce"], gbcc], writes=[sl["ngb"]])
    P.op("DVE", lambda h, pbR=pbR: h.scalar_tensor_tensor(
            sl["rb"][:, :], pbR.f128, sl["ngb"][:, 0:1], sl["vbb"][:, :],
            ALU.mult, ALU.add),
         reads=[pbR.key, sl["ngb"], sl["vbb"]], writes=[sl["rb"]])
    pbDl = nxt()
    P.op("PE", lambda h, pbDl=pbDl, TTinv=TTinv: h.matmul(
            pbDl.f128, TTinv[:, :], sl["rb"][:, :], start=True, stop=True),
         reads=[TTinv, sl["rb"]], writes=[pbDl.key])
    P.op("DVE", lambda h, pbDl=pbDl: h.tensor_copy(
            sl["db"][:, :], pbDl.f128),
         reads=[pbDl.key], writes=[sl["db"]])
    P.op("DVE", lambda h: h.tensor_mul(
            sl["qg"][:, :], qc, sl["Gam"][:, :]),
         reads=[qbh, sl["Gam"]], writes=[sl["qg"]])
    pbO = nxt()
    P.ops("PE", [
        lambda h, pbO=pbO: h.matmul(pbO.f128, sl["qg"][:, :], Sb[:, :],
                                    start=True, stop=False),
        lambda h, pbO=pbO: h.matmul(pbO.f128, sl["PTb"][:, :],
                                    sl["db"][:, :], start=False, stop=True),
    ], reads=[sl["qg"], Sb, sl["PTb"], sl["db"]], writes=[pbO.key])
    pbK = nxt()
    P.op("PE", lambda h, pbK=pbK: h.transpose(
            pbK.bf, kc, t.ident_bf[:, :]),
         reads=[kbh, t.ident_bf], writes=[pbK.key])
    P.op("DVE", lambda h, pbK=pbK: h.tensor_scalar_mul(
            sl["kdT"][:, :], pbK.bf,
            sl["Ein"][:, 127:128]),
         reads=[pbK.key, sl["Ein"]], writes=[sl["kdT"]])
    pbS = nxt()
    P.op("PE", lambda h, pbS=pbS: h.matmul(
            pbS.f128, sl["kdT"][:, :], sl["db"][:, :],
            start=True, stop=True),
         reads=[sl["kdT"], sl["db"]], writes=[pbS.key])
    P.op("DVE", lambda h: h.tensor_mul(
            sl["lam"][:, :], sl["Ein"][:, 127:128], sl["gce"][:, :]),
         reads=[sl["Ein"], sl["gce"]], writes=[sl["lam"]])
    P.op("DVE", lambda h, pbS=pbS: h.scalar_tensor_tensor(
            S[:, :], S[:, :], sl["lam"][:, 0:1], pbS.f128,
            ALU.mult, ALU.add),
         reads=[S, sl["lam"], pbS.key], writes=[S])
    P.op("DVE", lambda h: h.tensor_copy(Sb[:, :], S[:, :]),
         reads=[S], writes=[Sb])
    pbG = nxt()
    P.op("PE", lambda h, pbG=pbG: h.transpose(
            pbG.bf, gc, t.ident_bf[:, :]),
         reads=[gbh, t.ident_bf], writes=[pbG.key])
    P.op("ACT", lambda h, pbG=pbG: h.activation(
            sl["sgate"][:, :], pbG.bf, AF.Silu),
         reads=[pbG.key], writes=[sl["sgate"]])
    P.op("DVE", lambda h, pbO=pbO: h.tensor_mul(
            sl["om"][:, :], pbO.f128, sl["sgate"][:, :]),
         reads=[pbO.key, sl["sgate"]], writes=[sl["om"]])
    P.op("DVE", lambda h: h.tensor_mul(
            sl["sq2"][:, :], sl["om"][:, :], sl["om"][:, :]),
         reads=[sl["om"]], writes=[sl["sq2"]])
    P.op("DVE", lambda h: h.reduce_sum(
            sl["ms"][:, :], sl["sq2"][:, :], axis=AX.X),
         reads=[sl["sq2"]], writes=[sl["ms"]])
    P.op("ACT", lambda h: h.activation(
            sl["rstd"][:, :], sl["ms"][:, :], AF.Ln, scale=1.0 / 128,
            bias=t.cst[:, 2:3]),
         reads=[sl["ms"], (t.cst.name, 2)], writes=[(sl["rstd"].name, "l")])
    P.op("ACT", lambda h: h.activation(
            sl["rstd"][:, :], sl["rstd"][:, :], AF.Exp, scale=-0.5),
         reads=[(sl["rstd"].name, "l")], writes=[sl["rstd"]])
    P.op("DVE", lambda h: h.tensor_scalar_mul(
            sl["on"][:, :], sl["om"][:, :], sl["rstd"][:, 0:1]),
         reads=[sl["om"], sl["rstd"]], writes=[sl["on"]])
    P.op("DVE", lambda h: h.tensor_mul(
            sl["on2"][:, :], sl["on"][:, :], onw_rep[:, :]),
         reads=[sl["on"], onw_rep], writes=[sl["on2"]])
    pbN = nxt()
    P.op("PE", lambda h, pbN=pbN: h.transpose(
            pbN.bf, sl["on2"][:, :], t.ident_bf[:, :]),
         reads=[sl["on2"], t.ident_bf], writes=[pbN.key])
    P.op("DVE", lambda h, pbN=pbN, co=co: h.tensor_copy(
            ost[:, co:co + 128], pbN.bf),
         reads=[pbN.key], writes=[(ost.name, co)])


def emit_out_partial(P, nc, cfg, t, tag, w_dram, oT_dram, bias_dram=None):
    T, D = cfg.T, cfg.D
    NQ = max(1, T // 512)
    Wq = min(512, T)
    o3 = oT_dram.ap().rearrange("(c p) t -> p c t", p=128)
    with P.block(f"wo_{tag}"), P.buffers() as bf:
        w_sb = bf.sbuf(f"{tag}_wsb", [128, 8 * D], BF16)
        w3 = w_sb[:, :].rearrange("p (c k) -> p c k", c=8)
        P.dma("qPOOL", w3, w_dram.ap().rearrange("(c p) k -> p c k", p=128),
              ch="wo", writes=[w_sb])
        ov = [bf.sbuf(f"{tag}_ov{i}", [128, T], BF16) for i in range(8)]
        for vc in range(8):
            P.dma("qSP", ov[vc][:, :], o3[:, vc, :], ch=f"ov{vc % 2}",
                  writes=[ov[vc]])
        stg = [bf.sbuf(f"{tag}_st{i}", [128, 512], F32) for i in range(2)]
        if bias_dram is not None:
            bias_sb = bf.sbuf(f"{tag}_bias", [128, 16], F32)
            P.dma("qSP", bias_sb[:, :], bias_dram[:, :],
                  ch="wo2", writes=[bias_sb])
        for dt in range(16):
            for w in range(NQ):
                pbk = t.pb[(dt * NQ + w) % 2]
                P.ops("PE", [lambda h, vc=vc, dt=dt, w=w, pbk=pbk: h.matmul(
                        pbk[:, 0:Wq], w3[:, vc, dt * 128:(dt + 1) * 128],
                        ov[vc][:, w * Wq:(w + 1) * Wq],
                        start=(vc == 0), stop=(vc == 7)) for vc in range(8)],
                      reads=[w_sb] + [ov[vc] for vc in range(8)], writes=[pbk])
                sg = stg[w % 2]
                if bias_dram is not None:
                    P.op("DVE", lambda h, sg=sg, pbk=pbk, dt=dt:
                         h.tensor_scalar_add(sg[:, 0:Wq], pbk[:, 0:Wq],
                                             bias_sb[:, dt:dt + 1]),
                         reads=[pbk, bias_sb], writes=[sg])
                else:
                    P.op("DVE", lambda h, sg=sg, pbk=pbk: h.tensor_copy(
                            sg[:, 0:Wq], pbk[:, 0:Wq]),
                         reads=[pbk], writes=[sg])
                q = (w * Wq) // cfg.ARCH
                P.dma("qACT", t.ar_in[q, dt * 128:(dt + 1) * 128,
                                      (w * Wq) % cfg.ARCH:
                                      (w * Wq) % cfg.ARCH + Wq],
                      sg[:, 0:Wq], reads=[sg], ch=f"arw{w % 2}",
                      writes=[(t.ar_in.name, (q, dt, w))])


def emit_allreduce(P, nc, cfg, t, tag):
    with P.block(f"ar_{tag}"):
        for q in range(cfg.NAR):
            P.cc(lambda h, q=q: h.collective_compute(
                    "AllReduce", ALU.add,
                    replica_groups=[[0, 1], [2, 3], [4, 5], [6, 7]],
                    ins=[t.ar_in[q].opt()], outs=[t.ar_out[q].opt()]),
                 reads=[(t.ar_in.name, "*")], writes=[(t.ar_out.name, q)])


def emit_add(P, nc, cfg, t, src, dst, tag):
    T = cfg.T
    with P.block(f"add_{tag}"), P.buffers() as bf:
        xf = [bf.sbuf(f"{tag}_axf{i}", [128, T], F32) for i in range(2)]
        xg = [bf.sbuf(f"{tag}_axg{i}", [128, T], F32) for i in range(2)]
        for dc in range(16):
            xft, xgt = xf[dc % 2], xg[dc % 2]
            P.dma("qSP", xft[:, 0:T], src[dc * 128:(dc + 1) * 128, :],
                  ch=f"axf{dc % 2}", writes=[xft])
            for q in range(cfg.NAR):
                P.dma("qACT", xgt[:, q * cfg.ARCH:(q + 1) * cfg.ARCH],
                      t.ar_out[q, dc * 128:(dc + 1) * 128, :],
                      reads=[(t.ar_out.name, q)],
                      ch=f"axg{dc % 2}", writes=[(xgt.name, q)])
            P.op("DVE", lambda h, xft=xft, xgt=xgt: h.tensor_add(
                    xft[:, 0:T], xft[:, 0:T], xgt[:, 0:T]),
                 reads=[xft, xgt] + [(xgt.name, q) for q in range(cfg.NAR)],
                 writes=[xft])
            P.dma("qSP", dst[dc * 128:(dc + 1) * 128, :], xft[:, 0:T],
                  reads=[xft], ch=f"axf{dc % 2}", writes=[(dst.name, dc)])


def emit_router(P, nc, cfg, t, comb_sb, tag):
    T = cfg.T
    NC = T // 128
    with P.block(f"router_{tag}"), P.buffers() as bf:
        def buf(name, width=16):
            return bf.sbuf(f"{tag}_{name}", [128, NC * width], F32)
        lg, sc, ch, tmp, m6, eq, cs0, cs1 = (
            buf(n) for n in ("lg", "sc", "ch", "tmp", "m6", "eq", "cs0", "cs1"))
        p8, p8b = buf("p8", 8), buf("p8b", 8)
        p4, p4b, p4c = buf("p4", 4), buf("p4b", 4), buf("p4c", 4)
        c1, c1b, c1c = buf("c1", 1), buf("c1b", 1), buf("c1c", 1)
        ones16 = bf.sbuf(f"{tag}_ones16", [128, 16], F32)
        bias_rep = bf.sbuf(f"{tag}_brep", [128, 16], F32)
        tie_rep = bf.sbuf(f"{tag}_trep", [128, 16], F32)

        def v3(b, w=16):
            return b[:, :].rearrange("p (c e) -> p c e", c=NC)

        def v4(b, w, x):
            return b[:, :].rearrange("p (c a b) -> p c a b", c=NC, b=x)

        P.op("POOL", lambda h: h.memset(ones16[:, :], 1.0), writes=[ones16])
        for (rep, srcd) in ((bias_rep, t.router_bias), (tie_rep, t.tie_eps)):
            rowb = bf.sbuf(f"{tag}_row_{srcd.name}", [1, 16], F32)
            P.dma("qSP", rowb[:, :], srcd[:, :], ch="rt", writes=[rowb])
            P.op("PE", lambda h, rowb=rowb: h.matmul(
                    t.ps_rep[:, 0:16], t.ones_row1[:, :], rowb[0:1, :],
                    start=True, stop=True),
                 reads=[rowb, t.ones_row1], writes=[t.ps_rep])
            P.op("DVE", lambda h, rep=rep: h.tensor_copy(
                    rep[:, :], t.ps_rep[:, 0:16]),
                 reads=[t.ps_rep], writes=[rep])

        def D(build, reads, writes):
            P.op("DVE", build, reads=reads, writes=writes)

        for c in range(NC):
            psh = t.pb[c % 2]
            P.op("PE", lambda h, c=c, psh=psh: h.transpose(
                    psh[:, 0:16], t.ab_sb[64:80, c * 128:(c + 1) * 128],
                    t.ident_f[64:80, 64:80]),
                 reads=[(t.ab_sb.name, "*"), t.ident_f], writes=[psh])
            P.op("DVE", lambda h, c=c, psh=psh: h.tensor_copy(
                    lg[:, c * 16:(c + 1) * 16], psh[:, 0:16]),
                 reads=[psh], writes=[(lg.name, c)])
        lgall = [(lg.name, c) for c in range(NC)]
        P.op("ACT", lambda h: h.activation(sc[:, :], lg[:, :], AF.Sigmoid),
             reads=lgall, writes=[sc])
        D(lambda h: h.tensor_add(
                v3(ch), v3(sc),
                bias_rep[:, :].rearrange("p (o e) -> p o e", o=1)
                .broadcast_to([128, NC, 16])),
          [sc, bias_rep], [ch])
        def vg(b):
            return b[:, :].rearrange("p (c e) -> p c e", e=4)

        def vg1(b):
            return b[:, :].rearrange("p (c e) -> p c e", e=1)

        D(lambda h: h.tensor_reduce(vg1(p4), vg(ch), AX.X, ALU.max),
          [ch], [(p4.name, "m1")])
        D(lambda h: h.tensor_tensor(
                vg(eq), vg(ch), vg1(p4).broadcast_to([128, NC * 4, 4]),
                ALU.is_ge),
          [ch, (p4.name, "m1")], [(eq.name, "e1")])
        D(lambda h: h.scalar_tensor_tensor(
                vg(tmp), vg(eq), -1e30, vg(ch), ALU.mult, ALU.add),
          [(eq.name, "e1"), ch], [(tmp.name, "v2")])
        D(lambda h: h.tensor_reduce(vg1(p4b), vg(tmp), AX.X, ALU.max),
          [(tmp.name, "v2")], [(p4b.name, "m2")])
        D(lambda h: h.tensor_add(p4[:, :], p4[:, :], p4b[:, :]),
          [(p4.name, "m1"), (p4b.name, "m2")], [(p4.name, "gsc")])
        D(lambda h: h.tensor_reduce(v3(c1, 1), v3(p4, 4), AX.X, ALU.max),
          [(p4.name, "gsc")], [(c1.name, "g1")])
        D(lambda h: h.tensor_tensor(
                v3(p4c, 4), v3(p4, 4),
                v3(c1, 1).broadcast_to([128, NC, 4]), ALU.is_ge),
          [(p4.name, "gsc"), (c1.name, "g1")], [(p4c.name, "eg")])
        D(lambda h: h.scalar_tensor_tensor(
                v3(p4c, 4), v3(p4c, 4), -1e30, v3(p4, 4), ALU.mult, ALU.add),
          [(p4c.name, "eg"), (p4.name, "gsc")], [(p4c.name, "rest")])
        D(lambda h: h.tensor_reduce(v3(c1c, 1), v3(p4c, 4), AX.X, ALU.max),
          [(p4c.name, "rest")], [c1c])
        D(lambda h: h.tensor_tensor(
                v3(p4b, 4), v3(p4, 4),
                v3(c1c, 1).broadcast_to([128, NC, 4]), ALU.is_ge),
          [(p4.name, "gsc"), c1c], [(p4b.name, "gmask")])
        D(lambda h: h.tensor_copy(
                v4(tmp, 16, 4),
                p4b[:, :].rearrange("p (c g o) -> p c g o", c=NC, o=1)
                .broadcast_to([128, NC, 4, 4])),
          [(p4b.name, "gmask")], [(tmp.name, "smask")])
        D(lambda h: h.tensor_mul(tmp[:, :], tmp[:, :], ch[:, :]),
          [(tmp.name, "smask"), ch], [(tmp.name, "mc")])
        D(lambda h: h.tensor_add(
                v3(tmp), v3(tmp),
                tie_rep[:, :].rearrange("p (o e) -> p o e", o=1)
                .broadcast_to([128, NC, 16])),
          [(tmp.name, "mc"), tie_rep], [tmp])
        P.op("POOL", lambda h: h.memset(m6[:, :], 0.0), writes=[m6])
        for it in range(6):
            D(lambda h: h.tensor_reduce(v3(c1, 1), v3(tmp), AX.X, ALU.max),
              [tmp], [c1])
            D(lambda h: h.tensor_tensor(
                    v3(eq), v3(tmp),
                    v3(c1, 1).broadcast_to([128, NC, 16]), ALU.is_ge),
              [tmp, c1], [eq])
            for c in range(NC):
                D(lambda h, c=c: h.tensor_tensor_scan(
                        cs0[:, c * 16:(c + 1) * 16], ones16[:, :],
                        eq[:, c * 16:(c + 1) * 16], 0.0, ALU.mult, ALU.add),
                  [eq, ones16], [(cs0.name, c)])
            cur = cs0
            D(lambda h, cur=cur: h.tensor_single_scalar(
                    cur[:, :], cur[:, :], 1.0, ALU.is_le),
              [(cs0.name, c) for c in range(NC)], [cur])
            D(lambda h, cur=cur: h.tensor_mul(eq[:, :], eq[:, :], cur[:, :]),
              [eq, cur], [(eq.name, "sel")])
            D(lambda h: h.tensor_add(m6[:, :], m6[:, :], eq[:, :]),
              [m6, (eq.name, "sel")], [m6])
            D(lambda h: h.scalar_tensor_tensor(
                    tmp[:, :], eq[:, :], -1e30, tmp[:, :], ALU.mult, ALU.add),
              [(eq.name, "sel"), tmp], [tmp])
        D(lambda h: h.tensor_mul(m6[:, :], m6[:, :], sc[:, :]),
          [m6, sc], [(m6.name, "comb")])
        D(lambda h: h.tensor_reduce(v3(c1, 1), v3(m6), AX.X, ALU.add),
          [(m6.name, "comb")], [c1])
        D(lambda h: h.tensor_scalar_add(c1b[:, :], c1[:, :], 1e-20),
          [c1], [c1b])
        P.op("DVE", lambda h: h.reciprocal(c1c[:, :], c1b[:, :]),
             reads=[c1b], writes=[(c1c.name, "rc")])
        D(lambda h: h.tensor_scalar_mul(c1c[:, :], c1c[:, :], 2.5),
          [(c1c.name, "rc")], [c1c])
        D(lambda h: h.tensor_mul(
                v3(m6), v3(m6), v3(c1c, 1).broadcast_to([128, NC, 16])),
          [(m6.name, "comb"), c1c], [m6])
        P.op("POOL", lambda h: h.memset(comb_sb[:, :], 0.0), writes=[comb_sb])
        for c in range(NC):
            psh = t.pb[c % 2]
            P.op("PE", lambda h, c=c, psh=psh: h.transpose(
                    psh[0:16, 0:128], m6[:, c * 16:(c + 1) * 16],
                    t.ident_f[:, :]),
                 reads=[m6, t.ident_f], writes=[psh])
            P.op("DVE", lambda h, c=c, psh=psh: h.tensor_copy(
                    comb_sb[0:16, c * 128:(c + 1) * 128], psh[0:16, 0:128]),
                 reads=[psh], writes=[(comb_sb.name, c)])


def emit_ffn(P, nc, cfg, t, tag):
    T, D = cfg.T, cfg.D
    PT = cfg.ARCH
    NP = T // PT
    NTK = PT // 512 if PT >= 512 else 1
    TK = min(512, PT)
    hs3 = t.hsT.ap().rearrange("(c p) t -> p c t", p=128)
    groups = [(0, 1), (2, 3), (4, 5), (6, 7), (8,)]
    with P.block(f"ffn_{tag}"), P.buffers() as bf:
        hsq = bf.sbuf(f"{tag}_hsq", [128, 16 * PT], BF16)
        hsq3 = hsq[:, :].rearrange("p (c t) -> p c t", c=16)
        wA = [bf.sbuf(f"{tag}_wA{i}", [128, 2 * 16 * 128], BF16)
              for i in range(2)]
        inter = [bf.sbuf(f"{tag}_it{i}", [128, 11 * PT], BF16)
                 for i in range(2)]
        wB = [bf.sbuf(f"{tag}_wB{i}", [128, 11 * 1024], BF16)
              for i in range(2)]
        cmb = [bf.sbuf(f"{tag}_cmb{i}", [128, PT], F32) for i in range(2)]
        sgb = [bf.sbuf(f"{tag}_sg{i}", [128, 512], F32) for i in range(2)]
        stg = sgb
        for p in range(NP):
            p0 = p * PT
            P.dma("qSP", hsq3, hs3[:, :, p0:p0 + PT], ch="hsq",
                  writes=[hsq])
            for gi, grp in enumerate(groups):
                for si, e in enumerate(grp):
                    cme = cmb[si]
                    if e < 8:
                        for q in range(NTK):
                            pbk = t.pb[6]
                            P.op("PE", lambda h, e=e, q=q, pbk=pbk, p0=p0: h.matmul(
                                    pbk[:, 0:TK],
                                    t.sel16_sb[0:64, e * 128:(e + 1) * 128],
                                    t.comb_sb[0:64, p0 + q * TK:p0 + (q + 1) * TK],
                                    start=True, stop=True),
                                 reads=[t.sel16_sb, (t.comb_sb.name, "*")],
                                 writes=[pbk])
                            P.op("DVE", lambda h, q=q, cme=cme, pbk=pbk:
                                 h.tensor_copy(cme[:, q * TK:(q + 1) * TK],
                                               pbk[:, 0:TK]),
                                 reads=[pbk], writes=[(cme.name, q)])
                    itb = inter[si]
                    itb3 = itb[:, :].rearrange("p (c t) -> p c t", c=11)
                    for it in range(11):
                        wab = wA[it % 2]
                        wab4 = wab[:, :].rearrange(
                            "p (g c m) -> p g c m", g=2, c=16)
                        P.dma("qPOOL", wab4,
                              t.expA[e, :, it], ch=f"wa{it % 2}",
                              writes=[wab])
                        for q in range(NTK):
                            pbg = t.pb[0 if q % 2 == 0 else 2]
                            pbu = t.pb[1 if q % 2 == 0 else 3]
                            P.ops("PE", [lambda h, dc=dc, q=q, wab4=wab4,
                                         pbg=pbg: h.matmul(
                                    pbg[:, 0:TK], wab4[:, 0, dc, :],
                                    hsq3[:, dc, q * TK:(q + 1) * TK],
                                    start=(dc == 0), stop=(dc == 15))
                                    for dc in range(16)],
                                  reads=[wab, hsq], writes=[pbg])
                            P.ops("PE", [lambda h, dc=dc, q=q, wab4=wab4,
                                         pbu=pbu: h.matmul(
                                    pbu[:, 0:TK], wab4[:, 1, dc, :],
                                    hsq3[:, dc, q * TK:(q + 1) * TK],
                                    start=(dc == 0), stop=(dc == 15))
                                    for dc in range(16)],
                                  reads=[wab, hsq], writes=[pbu])
                            sg = sgb[q % 2]
                            ii = sg
                            P.op("ACT", lambda h, sg=sg, pbg=pbg: h.activation(
                                    sg[:, 0:TK], pbg[:, 0:TK], AF.Silu),
                                 reads=[pbg], writes=[sg])
                            P.op("DVE", lambda h, sg=sg, pbu=pbu:
                                 h.tensor_mul(sg[:, 0:TK], sg[:, 0:TK],
                                              pbu[:, 0:TK]),
                                 reads=[sg, pbu], writes=[sg])
                            if e < 8:
                                P.op("DVE", lambda h, ii=ii, cme=cme, q=q,
                                     it=it, itb3=itb3: h.tensor_mul(
                                        itb3[:, it, q * TK:(q + 1) * TK],
                                        ii[:, 0:TK],
                                        cme[:, q * TK:(q + 1) * TK]),
                                     reads=[ii, (cme.name, q)],
                                     writes=[(itb.name, (it, q))])
                            else:
                                P.op("DVE", lambda h, ii=ii, q=q, it=it,
                                     itb3=itb3: h.tensor_copy(
                                        itb3[:, it, q * TK:(q + 1) * TK],
                                        ii[:, 0:TK]),
                                     reads=[ii], writes=[(itb.name, (it, q))])
                for dhalf in range(2):
                    for si, e in enumerate(grp):
                        wbb = wB[si]
                        wbb3 = wbb[:, :].rearrange("p (c n) -> p c n", c=11)
                        P.dma("qPOOL", wbb3,
                              t.expB[e, :, :, dhalf * 1024:(dhalf + 1) * 1024],
                              ch=f"wb{si}", writes=[wbb])
                    for dt in range(8):
                        for q in range(NTK):
                            pbk = t.pb[4 + (dt * NTK + q) % 2]
                            mms = []
                            for si in range(len(grp)):
                                wbb3 = wB[si][:, :].rearrange(
                                    "p (c n) -> p c n", c=11)
                                itb3 = inter[si][:, :].rearrange(
                                    "p (c t) -> p c t", c=11)
                                for ic in range(11):
                                    first = (si == 0 and ic == 0)
                                    last = (si == len(grp) - 1 and ic == 10)
                                    mms.append(
                                        lambda h, si=si, ic=ic, dt=dt, q=q,
                                        wbb3=wbb3, itb3=itb3, first=first,
                                        last=last, pbk=pbk: h.matmul(
                                            pbk[:, 0:TK],
                                            wbb3[:, ic, dt * 128:(dt + 1) * 128],
                                            itb3[:, ic, q * TK:(q + 1) * TK],
                                            start=first, stop=last))
                            P.ops("PE", mms,
                                  reads=[wB[si] for si in range(len(grp))] +
                                        [inter[si] for si in range(len(grp))] +
                                        [(inter[si].name, (ic, q))
                                         for si in range(len(grp))
                                         for ic in range(11)],
                                  writes=[pbk])
                            sgt = stg[q % 2]
                            P.op("DVE", lambda h, sgt=sgt, pbk=pbk:
                                 h.tensor_copy(sgt[:, 0:TK], pbk[:, 0:TK]),
                                 reads=[pbk], writes=[sgt])
                            dr = dhalf * 8 + dt
                            P.dma("qPOOL",
                                  t.ar_in[p, dr * 128:(dr + 1) * 128,
                                          q * TK:(q + 1) * TK],
                                  sgt[:, 0:TK], reads=[sgt],
                                  ch=f"acc{(dt + q) % 2}",
                                  writes=[(t.ar_in.name, (p, dr, q))],
                                  **({} if gi == 0 else
                                     {"accum_op": ALU.add}))


def emit_attn_b(P, nc, cfg, t):
    T = cfg.T
    NQ = max(1, T // 512)
    Wq = min(512, T)
    scale = 1.0 / math.sqrt(128.0)
    qkv3 = t.qkvT.ap().rearrange("(c p) t -> p c t", p=128)
    oT3 = t.oT.ap().rearrange("(c p) t -> p c t", p=128)
    with P.block("attnb"), P.buffers() as bf:
        cmask = bf.sbuf("cmask", [128, 4 * Wq], F32)
        cm3 = cmask[:, :].rearrange("p (a w) -> p a w", a=4)
        P.dma("qSP", cm3,
              t.causal_mask.ap().rearrange("(a p) w -> p a w", p=128)
              [:, :, 0:Wq],
              ch="cm", writes=[cmask])
        qs = [bf.sbuf(f"ab_q{i}", [128, T], BF16) for i in range(2)]
        ks = [bf.sbuf(f"ab_k{i}", [128, T], BF16) for i in range(2)]
        vss = [bf.sbuf(f"ab_vs{i}", [128, T], BF16) for i in range(2)]
        vTs = [bf.sbuf(f"ab_vT{i}", [128, T], BF16) for i in range(2)]
        ps_ = [bf.sbuf(f"ab_p{i}", [128, Wq], BF16) for i in range(4)]
        sm_ = [bf.sbuf(f"ab_sm{i}", [128, Wq], F32) for i in range(2)]
        rr_ = [bf.sbuf(f"ab_rr{i}", [1, Wq], F32) for i in range(2)]
        rp_ = [bf.sbuf(f"ab_rp{i}", [128, Wq], F32) for i in range(2)]
        og_ = [bf.sbuf(f"ab_og{i}", [128, Wq], BF16) for i in range(2)]
        for bh in range(8):
            sl = bh % 2
            qb, kb, vs, vT = qs[sl], ks[sl], vss[sl], vTs[sl]
            P.dma("qSP", qb[:, :], qkv3[:, bh, :], ch=f"abq{sl}", writes=[qb])
            P.dma("qSP", kb[:, :], qkv3[:, 8 + bh, :], ch=f"abk{sl}",
                  writes=[kb])
            P.dma("qACT", vs[:, :], qkv3[:, 16 + bh, :], ch=f"abv{sl}",
                  writes=[vs])
            for kt in range(T // 128):
                pbv = t.pb[4 + sl]
                P.op("PE", lambda h, kt=kt, pbv=pbv, vs=vs: h.transpose(
                        pbf(pbv), vs[:, kt * 128:(kt + 1) * 128],
                        t.ident_bf[:, :]),
                     reads=[vs, t.ident_bf], writes=[pbv])
                P.op("DVE", lambda h, kt=kt, pbv=pbv, vT=vT: h.tensor_copy(
                        vT[:, kt * 128:(kt + 1) * 128], pbf(pbv)),
                     reads=[pbv], writes=[(vT.name, kt)])
            for qt in range(NQ):
                nk = (qt * Wq + Wq) // 128
                pbo = t.pb[0 + sl]
                pbr = t.pb[2 + sl]
                for kt in range(nk):
                    pbs = t.pb[6 + (kt % 2)]
                    P.op("PE", lambda h, kt=kt, qt=qt, pbs=pbs, kb=kb, qb=qb:
                         h.matmul(pbs[:, 0:Wq], kb[:, kt * 128:(kt + 1) * 128],
                                  qb[:, qt * Wq:(qt + 1) * Wq],
                                  start=True, stop=True),
                         reads=[kb, qb], writes=[pbs])
                    diag = kt >= (qt * Wq) // 128
                    pe = ps_[2 * sl + (kt % 2)]
                    if diag:
                        a = kt - (qt * Wq) // 128
                        smb = sm_[sl]
                        P.op("DVE", lambda h, pbs=pbs, smb=smb, a=a:
                             h.tensor_add(smb[:, 0:Wq], pbs[:, 0:Wq],
                                          cm3[:, a, 0:Wq]),
                             reads=[pbs, cmask], writes=[smb])
                        P.op("ACT", lambda h, smb=smb, pe=pe: h.activation(
                                pe[:, 0:Wq], smb[:, 0:Wq], AF.Exp,
                                scale=scale),
                             reads=[smb], writes=[pe])
                    else:
                        P.op("ACT", lambda h, pbs=pbs, pe=pe: h.activation(
                                pe[:, 0:Wq], pbs[:, 0:Wq], AF.Exp,
                                scale=scale),
                             reads=[pbs], writes=[pe])
                    P.op("PE", lambda h, pe=pe, pbr=pbr, kt=kt, nk=nk: h.matmul(
                            pbr[0:1, 0:Wq], t.ones_col[:, :], pe[:, 0:Wq],
                            start=(kt == 0), stop=(kt == nk - 1)),
                         reads=[pe, t.ones_col],
                         writes=[pbr] if kt == nk - 1 else [(pbr.name, kt)])
                    P.op("PE", lambda h, pe=pe, pbo=pbo, kt=kt, vT=vT, nk=nk:
                         h.matmul(pbo[:, 0:Wq],
                                  vT[:, kt * 128:(kt + 1) * 128],
                                  pe[:, 0:Wq],
                                  start=(kt == 0), stop=(kt == nk - 1)),
                         reads=[pe, vT],
                         writes=[pbo] if kt == nk - 1 else [(pbo.name, kt)])
                rr = rr_[sl]
                P.op("DVE", lambda h, rr=rr, pbr=pbr: h.reciprocal(
                        rr[0:1, 0:Wq], pbr[0:1, 0:Wq]),
                     reads=[pbr], writes=[rr])
                pbx = t.pb[6]
                P.op("PE", lambda h, rr=rr, pbx=pbx: h.matmul(
                        pbx[:, 0:Wq], t.ones_row1[:, :], rr[0:1, 0:Wq],
                        start=True, stop=True),
                     reads=[rr, t.ones_row1], writes=[pbx])
                rp = rp_[sl]
                P.op("DVE", lambda h, rp=rp, pbx=pbx: h.tensor_copy(
                        rp[:, 0:Wq], pbx[:, 0:Wq]),
                     reads=[pbx], writes=[rp])
                og = og_[sl]
                P.op("DVE", lambda h, og=og, pbo=pbo, rp=rp: h.tensor_mul(
                        og[:, 0:Wq], pbo[:, 0:Wq], rp[:, 0:Wq]),
                     reads=[pbo, rp], writes=[og])
                P.dma("qACT", oT3[:, bh, qt * Wq:(qt + 1) * Wq], og[:, 0:Wq],
                      reads=[og], ch=f"abo{sl}", writes=[(t.oT.name, (bh, qt))])


def emit_tap(P, nc, cfg, t, name, src, shape, dtype=F32):
    tap = nc.dram_tensor(f"tap_{name}", shape, dtype, kind="ExternalOutput")
    t.taps[name] = tap
    P.dma("qPOOL", tap.ap(), src, ch="tap")


def build(cfg=None, taps=(), n_phases=99):
    cfg = cfg or Cfg()
    nc = bass.Bass("TRN2", target_bir_lowering=False, num_devices=8)
    t = declare(nc, cfg, taps)
    with contextlib.ExitStack() as st:
        P = Emitter(nc, st)
        alloc_common(nc, st, cfg, t)
        with P.block("consts"):
            emit_consts(P, t)

        negcs = st.enter_context(nc.sbuf_tensor("negcs", [128, 1], F32))
        ones_colf = st.enter_context(nc.sbuf_tensor("ones_colf", [128, 1], F32))
        with P.block("prep_ab"), P.buffers() as bfp:
            w32f = bfp.sbuf("w32f", [128, 16 * 128], F32)
            P.op("POOL", lambda h: h.memset(ones_colf[:, :], 1.0),
                 writes=[ones_colf])
            w32f3 = w32f[:, :].rearrange("p (c k) -> p c k", c=16)
            P.dma("qSP", w32f3,
                  t.w_ab128.ap().rearrange("(c p) k -> p c k", p=128),
                  ch="w32", writes=[w32f])
            P.ops("PE", [lambda h, dc=dc: h.matmul(
                    t.ps_ab[:, 0:1], w32f3[:, dc, :], ones_colf[:, :],
                    start=(dc == 0), stop=(dc == 15)) for dc in range(16)],
                  reads=[w32f, ones_colf], writes=[t.ps_ab])
            P.op("DVE", lambda h: h.tensor_scalar_mul(
                    negcs[:, :], t.ps_ab[:, 0:1], -1.0),
                 reads=[t.ps_ab], writes=[negcs])

        ab_base = {"w32_dram": t.w_ab128, "colsum_col": negcs, "ncol": 128}

        with P.buffers() as bsA:
            t.ab_sb = bsA.sbuf("ab_sb_a", [128, cfg.T], F32)
            emit_ln(P, nc, cfg, t, t.xT, t.hsT, "ln1",
                    ab=dict(ab_base, dst=t.ab_sb))
            emit_proj(P, nc, cfg, t, "qkva", t.hsT,
                      [(t.w_qkv, t.qkvT), (t.w_gate, t.gateT)])
            emit_conv_norm(P, nc, cfg, t)
            emit_delta(P, nc, cfg, t)
            if "comb" in taps:
                t.comb_sb = bsA.sbuf("comb_sb_t", [64, cfg.T], F32)
                emit_router(P, nc, cfg, t, t.comb_sb, "rt_test")
                with P.block("tap_comb"):
                    tapc = nc.dram_tensor("tap_comb", [64, cfg.T], F32,
                                          kind="ExternalOutput")
                    t.taps["comb"] = tapc
                    P.dma("qPOOL", tapc.ap(), t.comb_sb[:, :], ch="tap")
            if "ab" in taps:
                with P.block("tap_ab"):
                    tapa = nc.dram_tensor("tap_ab", [128, cfg.T], F32,
                                          kind="ExternalOutput")
                    t.taps["ab"] = tapa
                    P.dma("qPOOL", tapa.ap(), t.ab_sb[:, :], ch="tap")
        emit_out_partial(P, nc, cfg, t, "woa", t.w_o, t.oT)
        emit_allreduce(P, nc, cfg, t, "a1")
        emit_add(P, nc, cfg, t, t.xT, t.xA, "x1")

        if n_phases >= 2:
            with P.buffers() as bsB:
                t.comb_sb = bsB.sbuf("comb_sb_1", [64, cfg.T], F32)
                with P.buffers() as bab:
                    t.ab_sb = bab.sbuf("ab_sb_1", [128, cfg.T], F32)
                    emit_ln(P, nc, cfg, t, t.xA, t.hsT, "ln2",
                            ab=dict(ab_base, dst=t.ab_sb))
                    emit_router(P, nc, cfg, t, t.comb_sb, "rt1")
                    if "ab2" in taps:
                        with P.block("tap_ab2"):
                            tapa2 = nc.dram_tensor("tap_ab2", [128, cfg.T],
                                                   F32, kind="ExternalOutput")
                            t.taps["ab2"] = tapa2
                            P.dma("qPOOL", tapa2.ap(), t.ab_sb[:, :], ch="tap")
                if "comb2" in taps:
                    with P.block("tap_comb2"):
                        tapc = nc.dram_tensor("tap_comb2", [64, cfg.T], F32,
                                              kind="ExternalOutput")
                        t.taps["comb2"] = tapc
                        P.dma("qPOOL", tapc.ap(), t.comb_sb[:, :], ch="tap")
                emit_ffn(P, nc, cfg, t, "ffn1")
            emit_allreduce(P, nc, cfg, t, "a2")
            if "hs2" in taps:
                with P.block("tap_hs2"):
                    emit_tap(P, nc, cfg, t, "hs2", t.hsT.ap(),
                             [cfg.D, cfg.T], BF16)
                    emit_tap(P, nc, cfg, t, "ar2", t.ar_out.ap(),
                             [cfg.NAR, cfg.D, cfg.ARCH], F32)
            emit_add(P, nc, cfg, t, t.xA, t.xB, "x2")

        if n_phases >= 3:
            emit_ln(P, nc, cfg, t, t.xB, t.hsT, "ln3")
            emit_proj(P, nc, cfg, t, "qkvb", t.hsT, [(t.w_attn, t.qkvT)],
                      bias_sb=t.b_attn_sb)
            emit_attn_b(P, nc, cfg, t)
            emit_out_partial(P, nc, cfg, t, "wob", t.w_proj, t.oT,
                             bias_dram=t.b_proj)
            emit_allreduce(P, nc, cfg, t, "a3")
            emit_add(P, nc, cfg, t, t.xB, t.xA, "x3")

        if n_phases >= 4:
            with P.buffers() as bsC:
                t.comb_sb = bsC.sbuf("comb_sb_2", [64, cfg.T], F32)
                with P.buffers() as bab:
                    t.ab_sb = bab.sbuf("ab_sb_2", [128, cfg.T], F32)
                    emit_ln(P, nc, cfg, t, t.xA, t.hsT, "ln4",
                            ab=dict(ab_base, dst=t.ab_sb))
                    emit_router(P, nc, cfg, t, t.comb_sb, "rt2")
                emit_ffn(P, nc, cfg, t, "ffn2")
            emit_allreduce(P, nc, cfg, t, "a4")
            emit_add(P, nc, cfg, t, t.xA, t.out_xT, "x4")

        with P.block("taps"):
            if "hs1" in taps:
                emit_tap(P, nc, cfg, t, "hs1", t.hsT.ap(), [cfg.D, cfg.T], BF16)
            if "qkvT" in taps:
                emit_tap(P, nc, cfg, t, "qkvT", t.qkvT.ap(), [3072, cfg.T], BF16)
                emit_tap(P, nc, cfg, t, "gateT", t.gateT.ap(), [1024, cfg.T], BF16)
            if "qkvT2" in taps:
                emit_tap(P, nc, cfg, t, "qkvT2", t.qkvT2.ap(), [3072, cfg.T], BF16)
            if "oT" in taps:
                emit_tap(P, nc, cfg, t, "oT", t.oT.ap(), [1024, cfg.T], BF16)
            if "x1" in taps:
                emit_tap(P, nc, cfg, t, "x1", t.xA.ap(), [cfg.D, cfg.T], F32)
            if "x2" in taps:
                emit_tap(P, nc, cfg, t, "x2", t.xB.ap(), [cfg.D, cfg.T], F32)
            if "x3" in taps:
                emit_tap(P, nc, cfg, t, "x3", t.xA.ap(), [cfg.D, cfg.T], F32)

        with P.block("fin"):
            P.wait_all(["SP", "POOL", "ACT"])
    return nc, t



def kernel(**inputs):
    cfg = Cfg()
    in_maps = stage_inputs(inputs, cfg)
    nc, t = build(cfg)
    res = run_bass_kernel_spmd(nc, in_maps, core_ids=list(range(8)))
    outs = []
    for b in range(4):
        outs.append(np.asarray(res.results[2 * b]["out_xT"], np.float32).T)
    return np.stack(outs).astype(np.float32)
